# version 21
# baseline (speedup 1.0000x reference)
import numpy as np

import concourse.bass as bass
import concourse.mybir as mybir
from concourse.bass_utils import run_bass_kernel_spmd

F32 = mybir.dt.float32
BF16 = mybir.dt.bfloat16
AF = mybir.ActivationFunctionType
ALU = mybir.AluOpType
AX = mybir.AxisListType

BIG = 1.0e30


def default_cfg():
    return dict(dm=1024, di=2048, ds=64, f=4096, e=8, r=4,
                nt=4096, tile=512, ncores=8, nbatch=2)


class Sem:
    def __init__(self, h):
        self.h = h
        self.n = 0


class Builder:

    def __init__(self, racecheck=False):
        self.nc = bass.Bass("TRN2", target_bir_lowering=False,
                            detect_race_conditions=racecheck)
        self.q = {"t": [], "v": [], "s": [], "y": [], "g": []}
        self._ctx = []

    def ent(self, t):
        self._ctx.append(t)
        return t.__enter__()

    def sb(self, nm, shape, dt):
        return self.ent(self.nc.sbuf_tensor(nm, shape, dt))

    def ps(self, nm, shape, dt=F32):
        return self.ent(self.nc.psum_tensor(nm, shape, dt))

    def sem(self, nm):
        return Sem(self.ent(self.nc.semaphore(nm)))

    def op(self, eng, fn, inc=None, k=1):
        if inc is None:
            self.q[eng].append(fn)
            return None
        inc.n += k
        self.q[eng].append(lambda e, f=fn, s=inc.h, kk=k: f(e).then_inc(s, kk))
        return inc.n

    def wait(self, eng, sem, val=None):
        v = sem.n if val is None else val
        if v <= 0:
            return
        self.q[eng].append(lambda e, s=sem.h, vv=v: e.wait_ge(s, vv))

    def finalize(self):
        nc = self.nc
        q = self.q
        with nc.Block() as block:
            @block.tensor
            def _(e):
                for f in q["t"]:
                    f(e)

            @block.vector
            def _(e):
                for f in q["v"]:
                    f(e)

            @block.scalar
            def _(e):
                for f in q["s"]:
                    f(e)

            @block.sync
            def _(e):
                for f in q["y"]:
                    f(e)

            @block.gpsimd
            def _(e):
                for f in q["g"]:
                    f(e)
        for t in reversed(self._ctx):
            t.__exit__(None, None, None)
        return nc


def build_program(cfg):
    dm, di, ds, f, e_, r_ = (cfg[k] for k in ["dm", "di", "ds", "f", "e", "r"])
    nt, tile, ncores, nbatch = (cfg[k] for k in ["nt", "tile", "ncores", "nbatch"])
    stop = cfg.get("stop", "all")
    km = dm // 128
    dsl = di // 128
    fm = f // 128
    er = e_ * r_
    pw = 2 * ds + 1
    ntiles = nt // tile
    tsn = tile // 128
    nrow = nt // 128
    n512 = dm // 512

    B = Builder(racecheck=cfg.get("racecheck", False))
    nc = B.nc
    op, wait = B.op, B.wait

    def din(nm, shape, dt=F32):
        return nc.dram_tensor(nm, shape, dt, kind="ExternalInput").ap()

    x_in = din("x", [nt, dm])
    win_in = din("w_win", [128, km * 2 * di])
    wxp_in = din("w_wxp", [128, dsl * pw])
    wb_in = din("w_wb", [ds, di])
    wc_in = din("w_wc", [ds, di])
    wout_in = din("w_wout", [128, dsl * dm])
    wi_in = din("w_wi", [128, km * f])
    wo_in = din("w_wo", [128, fm * dm])
    gatet_in = din("w_gatet", [128, km * e_])
    lorat_in = din("w_lorat", [128, km * er])
    lorab_in = din("w_lorab", [er, f])
    dcol_in = din("w_dcol", [128, dsl])
    wibcol_in = din("w_wibcol", [128, fm])
    wobb_in = din("w_wobb", [128, dm])
    gateb_in = din("w_gateb", [128, e_])
    alog_in = din("w_alog", [1, 1])
    ident_in = din("w_ident", [128, 128])
    onesr_in = din("w_onesr", [1, 128])
    iotar_in = din("w_iotar", [128, e_])
    cmaske_in = din("w_cmaske", [128, ncores * ncores])
    cmaskm_in = din("w_cmaskm", [128, ncores])

    o_out = nc.dram_tensor("o", [nt, dm], F32, kind="ExternalOutput").ap()

    dbg = cfg.get("dbg", False)
    dbgkw = dict(kind="ExternalOutput") if dbg else {}
    sp_yl = nc.dram_tensor("sp_yl", [ntiles, 128, dsl * tile], BF16, **dbgkw).ap()
    sp_k = nc.dram_tensor("sp_k", [ntiles, 128, dsl * tile], BF16, **dbgkw).ap()
    sp_x1 = nc.dram_tensor("sp_x1", [nrow, 128, dm], F32, **dbgkw)
    sp_int = nc.dram_tensor("sp_int", [ntiles, 128, fm, tile], BF16, **dbgkw)
    sp_dbg = nc.dram_tensor("sp_dbg", [ntiles, tsn, 128, 4 * e_], F32,
                            **dbgkw) if dbg else None
    cc_in_t = nc.dram_tensor("cc_in", [128, dsl + 1], F32)
    cc_out_t = nc.dram_tensor("cc_out", [ncores, 128, dsl + 1], F32)

    acols = max(km * 2 * di, km * f)
    bcols = dsl * dm
    wo_in_b = min(fm, bcols // dm)
    wbig = B.sb("wbig", [128, acols + bcols], BF16)

    def win_ap(k):
        return wbig[:, k * 2 * di:(k + 1) * 2 * di]

    def wout_ap(k):
        return wbig[:, acols + k * dm:acols + (k + 1) * dm]

    def wi_ap(k):
        return wbig[:, k * f:(k + 1) * f]

    def wo_ap(k):
        if k < wo_in_b:
            return wbig[:, acols + k * dm:acols + (k + 1) * dm]
        kk = k - wo_in_b
        return wbig[:, kk * dm:(kk + 1) * dm]

    wxp = B.sb("wxp", [128, dsl * pw], BF16)
    wb = B.sb("wb", [ds, di], BF16)
    wc = B.sb("wc", [ds, di], BF16)
    lorab = B.sb("lorab", [er, f], BF16)
    lorat = B.sb("lorat", [128, km * er], BF16)
    gatet = B.sb("gatet", [128, km * e_], BF16)
    dcol = B.sb("dcol", [128, dsl], F32)
    wibcol = B.sb("wibcol", [128, fm], F32)
    wobb = B.sb("wobb", [128, dm], BF16)
    gateb = B.sb("gateb", [128, e_], F32)
    idb = B.sb("idb", [128, 128], BF16)
    idf = B.sb("idf", [128, 128], F32)
    onesr = B.sb("onesr", [1, 128], F32)
    onest = B.sb("onest", [1, tile], F32)
    alog_sb = B.sb("alog_sb", [1, 1], F32)
    acol = B.sb("acol", [128, 1], F32)
    iotar = B.sb("iotar", [128, e_], F32)
    cmaske = B.sb("cmaske", [128, ncores * ncores], F32)
    cmaskm = B.sb("cmaskm", [128, ncores], F32)
    ccsb = B.sb("ccsb", [128, dsl + 1], F32)
    agsb = B.sb("agsb", [128, ncores * (dsl + 1)], F32)
    hin = B.sb("hin", [128, dsl], F32)
    scst = B.sb("scst", [128, dsl], F32)
    lsst = B.sb("lsst", [1, 1], F32)
    larow = B.sb("larow", [1, tile], F32)
    lsrow = B.sb("lsrow", [1, tile], F32)
    umt = B.sb("umt", [er, 2 * tile], BF16)
    umb = B.sb("umb", [128, er], BF16)
    oh8 = B.sb("oh8", [128, e_], F32)
    eq8 = B.sb("eq8", [128, e_], F32)
    pr8 = B.sb("pr8", [128, e_], F32)
    pr8b = B.sb("pr8b", [128, e_], F32)
    lg8 = B.sb("lg8", [128, e_], F32)
    mx1 = B.sb("mx1", [128, 1], F32)
    cred = B.sb("cred", [128, 1], F32)
    cexp = B.sb("cexp", [128, 1], F32)
    ccoef = B.sb("ccoef", [128, 1], F32)
    ctmp8 = B.sb("ctmp8", [128, ncores], F32)
    ctmp16 = B.sb("ctmp16", [128, dsl], F32)

    r16 = {}
    off = 0
    for nm, sz in [("xiT", dsl * tile), ("ln0T", km * tile), ("lntok", dm),
                   ("bctB", tile), ("bctC", tile), ("bx", tile),
                   ("sz", 2 * tile), ("ylw", 4 * tile), ("kw", 4 * tile),
                   ("qb", tile), ("scrA", tile), ("scrB", tile), ("scrC", tile),
                   ("bbf", 2 * tile), ("cbf", 2 * tile)]:
        r16[nm] = off
        off += sz
    r16["ist"] = r16["scrA"]
    a16cols = max(off, fm * tile, 2 * dsl * tile + 2 * dsl * 128)
    a16 = B.sb("a16", [128, a16cols], BF16)

    def A16(nm, lo, sz):
        return a16[:, r16[nm] + lo:r16[nm] + lo + sz]

    yl_in_off = 0
    k_in_off = dsl * tile
    y2_off = 2 * dsl * tile

    r32 = {}
    off = 0
    for nm, sz in [("stat", 26), ("proj", pw + 3), ("fout", dm), ("lnb", dm),
                   ("U", max(tile + 2 * tile + 2 * dm, tsn * dm + dm))]:
        r32[nm] = off
        off += sz
    r32["sq"] = r32["lnb"]
    r32["abc"] = r32["U"]
    r32["hT"] = r32["U"] + tile
    r32["xb"] = r32["U"] + 3 * tile
    r32["x1"] = r32["U"]
    r32["outb"] = r32["U"] + tsn * dm
    a32 = B.sb("a32", [128, off], F32)

    def A32(nm, lo, sz):
        return a32[:, r32[nm] + lo:r32[nm] + lo + sz]

    p = [B.ps("p%d" % i, [128, 512]) for i in range(7)]
    pT = B.ps("pT", [128, 512], BF16)
    epsc = B.sb("epsc", [128, 1], F32)

    ycn = B.sem("ycn")
    yx = [B.sem("yx0"), B.sem("yx1")]
    yyk = B.sem("yyk")
    yxl = B.sem("yxl")
    yil = B.sem("yil")
    yst = B.sem("yst")
    gw = B.sem("gw")
    gs = [B.sem("gs0"), B.sem("gs1")]
    gi = [B.sem("gi0"), B.sem("gi1")]
    gcc = B.sem("gcc")
    gP = B.sem("gP")
    tS = B.sem("tS")
    vS = B.sem("vS")
    sS = B.sem("sS")
    cS = B.sem("cS")


    def emit_ln_stats(srcs, pre_wait=None, m_slot=4, r_slot=9, defer_rstd=False):
        if pre_wait is not None:
            pre_wait()
        is_psum = bool(srcs and getattr(srcs[0][0].tensor, "name", "").startswith("p"))
        for i, (ap_, w_) in enumerate(srcs):
            op("v", lambda e, ap_=ap_, i=i: e.tensor_reduce(
                A32("stat", 0 + i, 1), ap_, AX.X, ALU.add), inc=vS)
        if is_psum:
            wait("s", vS)
            for i, (ap_, w_) in enumerate(srcs):
                op("s", lambda e, ap_=ap_, w_=w_, i=i: e.activation(
                    A32("sq", 0, w_), ap_, AF.Square,
                    accum_out=A32("stat", 2 + i, 1)), inc=sS)
            reads_done = (vS.n, sS.n)
            wait("v", sS)
        else:
            for i, (ap_, w_) in enumerate(srcs):
                op("v", lambda e, ap_=ap_, w_=w_, i=i: e.tensor_tensor(
                    A32("sq", 0, w_), ap_, ap_, ALU.mult), inc=vS)
                op("v", lambda e, w_=w_, i=i: e.tensor_reduce(
                    A32("stat", 2 + i, 1), A32("sq", 0, w_), AX.X, ALU.add), inc=vS)
            reads_done = (vS.n, 0)
        wait("v", vS)
        if len(srcs) == 2:
            op("v", lambda e: e.tensor_tensor(A32("stat", 0, 1), A32("stat", 0, 1), A32("stat", 1, 1), ALU.add), inc=vS)
            op("v", lambda e: e.tensor_tensor(A32("stat", 2, 1), A32("stat", 2, 1), A32("stat", 3, 1), ALU.add), inc=vS)
            wait("v", vS)
        op("v", lambda e, m_slot=m_slot: e.tensor_scalar(A32("stat", m_slot, 1), A32("stat", 0, 1), 1.0 / dm, None, ALU.mult), inc=vS)
        op("v", lambda e: e.tensor_scalar(A32("stat", 5, 1), A32("stat", 2, 1), 1.0 / dm, None, ALU.mult), inc=vS)
        wait("v", vS)
        op("v", lambda e, m_slot=m_slot: e.tensor_tensor(A32("stat", 6, 1), A32("stat", m_slot, 1), A32("stat", m_slot, 1), ALU.mult), inc=vS)
        wait("v", vS)
        if defer_rstd:
            op("v", lambda e, r_slot=r_slot: e.tensor_scalar(
                A32("stat", r_slot, 1), A32("stat", 5, 1), A32("stat", 6, 1), 1e-5,
                ALU.subtract, ALU.add), inc=vS)
            return reads_done
        op("v", lambda e: e.tensor_scalar(A32("stat", 7, 1), A32("stat", 5, 1), A32("stat", 6, 1), 1e-5,
                                          ALU.subtract, ALU.add), inc=vS)
        wait("s", vS)
        op("s", lambda e: e.activation(A32("stat", 7, 1), A32("stat", 7, 1), AF.Ln), inc=sS)
        wait("s", sS)
        op("s", lambda e, r_slot=r_slot: e.activation(A32("stat", r_slot, 1), A32("stat", 7, 1), AF.Exp, scale=-0.5), inc=sS)
        return reads_done

    def barrier():
        tv, vv, sv = tS.n, vS.n, sS.n
        for eng in ("t", "v", "s", "y", "g"):
            wait(eng, tS, tv)
            wait(eng, vS, vv)
            wait(eng, sS, sv)

    for nm, dst, src in [("idf", idf, ident_in), ("onesr", onesr, onesr_in),
                         ("dcol", dcol, dcol_in), ("wibcol", wibcol, wibcol_in),
                         ("gateb", gateb, gateb_in), ("iotar", iotar, iotar_in),
                         ("alog", alog_sb, alog_in), ("cmaske", cmaske, cmaske_in),
                         ("cmaskm", cmaskm, cmaskm_in)]:
        op("y", lambda e, d=dst, s=src: e.dma_start(out=d[:], in_=s), inc=ycn, k=16)
    op("g", lambda e: e.dma_start(out=wobb[:], in_=wobb_in), inc=gw, k=16)
    op("g", lambda e: e.dma_start(out=wbig[:, 0:km * 2 * di], in_=win_in), inc=gw, k=16)
    op("g", lambda e: e.dma_start(out=wxp[:], in_=wxp_in), inc=gw, k=16)
    op("g", lambda e: e.dma_start(out=wb[:], in_=wb_in), inc=gw, k=16)
    op("g", lambda e: e.dma_start(out=wc[:], in_=wc_in), inc=gw, k=16)
    op("g", lambda e: e.dma_start(out=idb[:], in_=ident_in), inc=gw, k=16)
    op("g", lambda e: e.dma_start(out=gatet[:], in_=gatet_in), inc=gw, k=16)
    op("g", lambda e: e.dma_start(out=lorat[:], in_=lorat_in), inc=gw, k=16)
    op("g", lambda e: e.dma_start(out=lorab[:], in_=lorab_in), inc=gw, k=16)
    for k in range(dsl):
        op("g", lambda e, k=k: e.dma_start(
            out=wout_ap(k), in_=wout_in[:, k * dm:(k + 1) * dm]), inc=gw, k=16)

    wait("s", ycn)
    op("s", lambda e: e.activation(alog_sb[:], alog_sb[:], AF.Exp), inc=sS)
    wait("v", sS)
    op("v", lambda e: e.tensor_scalar(alog_sb[:], alog_sb[:], -1.0, None, ALU.mult), inc=vS)
    op("v", lambda e: e.memset(onest[:], 1.0), inc=vS)
    op("v", lambda e: e.memset(epsc[:], 1e-5), inc=vS)
    op("v", lambda e: e.memset(scst[:], 0.0), inc=vS)
    op("v", lambda e: e.memset(lsst[:], 0.0), inc=vS)
    op("v", lambda e: e.memset(hin[:], 0.0), inc=vS)
    wait("t", vS)
    op("t", lambda e: e.matmul(p[6][:, 0:1], onesr[:], alog_sb[:], start=True, stop=True), inc=tS)
    wait("s", tS)
    op("s", lambda e: e.copy(acol[:], p[6][:, 0:1]), inc=sS)

    wait("t", gw)
    wait("v", gw)
    wait("s", gw)
    wait("t", ycn)
    wait("v", ycn)
    wait("s", ycn)
    barrier()

    xb_guard = {}
    ln_stat_v = {}
    p1_zmm_t = {}
    p1_vend = {}

    def LN_load(t, ts):
        row = t * tile + ts * 128
        xb = A32("xb", (ts % 2) * dm, dm)
        g = xb_guard.get(ts % 2)
        if g:
            wait("y", vS, g)
        op("y", lambda e, row=row, xb=xb: e.dma_start(
            out=xb, in_=x_in[row:row + 128, :]), inc=yx[ts % 2], k=16)

    def LN_stats(t, ts):
        xb = A32("xb", (ts % 2) * dm, dm)

        def _w(ts=ts):
            wait("v", yx[ts % 2])
        emit_ln_stats([(xb, dm)], pre_wait=_w, m_slot=16 + 2 * ts, r_slot=17 + 2 * ts,
                      defer_rstd=True)
        xb_guard[ts % 2] = vS.n
        ln_stat_v[(t, ts)] = (vS.n, sS.n)

    def LN_finish(t):
        for ts in range(tsn):
            wait("s", vS, ln_stat_v[(t, ts)][0])
            op("s", lambda e, ts=ts: e.activation(A32("stat", 17 + 2 * ts, 1),
                                                  A32("stat", 17 + 2 * ts, 1), AF.Ln), inc=sS)
            wait("s", sS)
            op("s", lambda e, ts=ts: e.activation(A32("stat", 17 + 2 * ts, 1),
                                                  A32("stat", 17 + 2 * ts, 1), AF.Exp, scale=-0.5), inc=sS)
        rstd_s = sS.n
        tr_done = 0
        for ts in range(tsn):
            xb = A32("xb", (ts % 2) * dm, dm)
            row = t * tile + ts * 128
            g = xb_guard.get(ts % 2)
            if g:
                wait("y", vS, g)
            op("y", lambda e, row=row, xb=xb: e.dma_start(
                out=xb, in_=x_in[row:row + 128, :]), inc=yx[ts % 2], k=16)
            wait("v", yx[ts % 2])
            wait("v", sS, rstd_s)
            if tr_done:
                wait("v", tS, tr_done)
            op("v", lambda e, xb=xb, ts=ts: e.tensor_scalar(
                A16("lntok", 0, dm), xb, A32("stat", 16 + 2 * ts, 1), A32("stat", 17 + 2 * ts, 1),
                ALU.subtract, ALU.mult), inc=vS)
            xb_guard[ts % 2] = vS.n
            wait("t", vS)
            if t > 0 and ts == 0:
                pass
            for g0 in range(0, km, 4):
                gn = min(4, km - g0)
                for j in range(gn):
                    op("t", lambda e, j=j, g0=g0: e.transpose(
                        pT[:, j * 128:(j + 1) * 128],
                        A16("lntok", (g0 + j) * 128, 128), idb[:]),
                        inc=tS if j == gn - 1 else None)
                wait("s", tS)
                if t > 0:
                    wait("s", tS, p1_zmm_t[t - 1])

                def evac_ln(e, g0=g0, gn=gn, ts=ts):
                    dst = a16[:, r16["ln0T"] + g0 * tile:
                              r16["ln0T"] + (g0 + gn) * tile].rearrange(
                        "p (k n) -> p k n", n=tile)[:, :, ts * 128:(ts + 1) * 128]
                    src = pT.rearrange("p (k n) -> p k n", n=128)[:, 0:gn, :]
                    return e.copy(dst, src)
                op("s", evac_ln, inc=sS)
                wait("t", sS)
            tr_done = tS.n

    for t in range(ntiles):
        if t == 0:
            for ts in range(tsn):
                LN_load(0, ts)
                LN_stats(0, ts)
        LN_finish(t)
        ln_done_s = sS.n

        wait("t", sS, ln_done_s)
        xi_evac = {}
        for m in range(dsl):
            px = p[m % 2]
            if m >= 2:
                wait("t", sS, xi_evac[m - 2])
            for k in range(km):
                op("t", lambda e, k=k, m=m, px=px: e.matmul(
                    px[:, 0:tile], win_ap(k)[:, m * 128:(m + 1) * 128],
                    A16("ln0T", k * tile, tile),
                    start=(k == 0), stop=(k == km - 1)),
                    inc=tS if k == km - 1 else None)
            wait("s", tS)
            if t > 0 and m < 2:
                wait("s", vS, p1_vend[t - 1])
            xi_evac[m] = op("s", lambda e, m=m, px=px: e.copy(
                A16("xiT", m * tile, tile), px[:, 0:tile]), inc=sS)

        wait("t", sS)
        for ts in range(tsn):
            for m in range(dsl):
                op("t", lambda e, m=m, ts=ts: e.matmul(
                    p[2][:, 0:pw],
                    a16[:, r16["xiT"] + m * tile + ts * 128:
                        r16["xiT"] + m * tile + ts * 128 + 128],
                    wxp[:, m * pw:(m + 1) * pw],
                    start=(m == 0), stop=(m == dsl - 1)),
                    inc=tS if m == dsl - 1 else None)
            wait("s", tS)
            op("s", lambda e: e.copy(A32("proj", 0, pw), p[2][:, 0:pw]), inc=sS)
            op("s", lambda e: e.activation(A32("stat", 8, 1), A32("proj", 0, 1), AF.Abs), inc=sS)
            wait("s", sS)
            op("s", lambda e: e.activation(A32("stat", 8, 1), A32("stat", 8, 1), AF.Exp, scale=-1.0), inc=sS)
            wait("s", sS)
            op("s", lambda e: e.activation(A32("stat", 8, 1), A32("stat", 8, 1), AF.Ln, bias=1.0), inc=sS)
            wait("v", sS)
            op("v", lambda e: e.scalar_tensor_tensor(
                A32("stat", 9, 1), A32("proj", 0, 1), 0.0, A32("stat", 8, 1),
                ALU.max, ALU.add), inc=vS)
            wait("v", vS)
            op("v", lambda e: e.tensor_scalar(
                A32("stat", 10, 1), A32("stat", 9, 1), acol[:, 0:1], None, ALU.mult), inc=vS)
            wait("v", vS)
            wait("t", vS)
            op("t", lambda e: e.transpose(p[6][0:1, 0:128], A32("stat", 10, 1), idf[:]), inc=tS)
            wait("s", tS)
            op("s", lambda e, ts=ts: e.copy(larow[0:1, ts * 128:(ts + 1) * 128],
                                            p[6][0:1, 0:128]), inc=sS)
            wait("t", sS)
            op("t", lambda e: e.transpose(p[6][0:ds, 0:128], A32("proj", 1, ds), idf[:]), inc=tS)
            op("t", lambda e: e.transpose(p[6][0:ds, 128:256], A32("proj", 1 + ds, ds), idf[:]), inc=tS)
            wait("s", tS)
            op("s", lambda e, ts=ts: e.copy(a16[0:ds, r16["bctB"] + ts * 128:r16["bctB"] + ts * 128 + 128],
                                            p[6][0:ds, 0:128]), inc=sS)
            op("s", lambda e, ts=ts: e.copy(a16[0:ds, r16["bctC"] + ts * 128:r16["bctC"] + ts * 128 + 128],
                                            p[6][0:ds, 128:256]), inc=sS)
            wait("t", sS)
        proj_done_s = sS.n

        wait("v", sS, proj_done_s)
        op("v", lambda e: e.tensor_tensor_scan(
            lsrow[:], onest[:], larow[:], lsst[:, 0:1], ALU.mult, ALU.add), inc=vS)
        wait("v", vS)
        op("v", lambda e: e.tensor_copy(lsst[:], lsrow[0:1, tile - 1:tile]), inc=vS)
        wait("v", vS)
        wait("t", vS)
        op("t", lambda e: e.matmul(p[5][:, 0:tile], onesr[:], larow[:], start=True, stop=True), inc=tS)
        wait("s", tS)
        if t > 0:
            wait("s", vS, p1_vend[t - 1])
        op("s", lambda e: e.activation(A32("abc", 0, tile), p[5][:, 0:tile], AF.Exp), inc=sS)
        wait("t", sS)
        op("t", lambda e: e.matmul(p[5][:, 0:tile], onesr[:], lsrow[:], start=True, stop=True), inc=tS)
        wait("s", tS)
        if t > 0:
            wait("s", gP)
        op("s", lambda e: e.activation(A16("qb", 0, tile), p[5][:, 0:tile], AF.Exp), inc=sS)
        aq_done_s = sS.n

        if t + 1 < ntiles and not cfg.get("noilv"):
            LN_load(t + 1, 0)
        wait("t", sS, aq_done_s)
        prev_silu = silu_done if t > 0 else {}
        prev_cev = cev if t > 0 else {}
        prev_vend = p1_vend.get(t - 1, 0)
        prev_pool = gP.n
        silu_done = {}
        bev = {}
        cev = {}
        ylk_done = {}
        pool_done = {}
        for m in range(dsl):
            pz = p[m % 2]
            pB = p[3] if m % 2 == 0 else p[5]
            pC = p[4] if m % 2 == 0 else p[6]
            if m == 0:
                wait("v", gs[0])
                wait("v", gs[1])
            if m >= 2:
                wait("t", sS, silu_done[m - 2])
                wait("t", sS, cev[m - 2])
                wait("s", vS, ylk_done[m - 2])
                wait("s", gP, pool_done[m - 2])
            elif t > 0:
                wait("t", sS, prev_silu[dsl - 2 + m])
                wait("t", sS, prev_cev[dsl - 2 + m])
                wait("s", vS, prev_vend)
                wait("s", gP, prev_pool)
            for k in range(km):
                op("t", lambda e, k=k, m=m, pz=pz: e.matmul(
                    pz[:, 0:tile], win_ap(k)[:, di + m * 128:di + (m + 1) * 128],
                    A16("ln0T", k * tile, tile),
                    start=(k == 0), stop=(k == km - 1)),
                    inc=tS if k == km - 1 else None)
            z_mm = tS.n
            op("t", lambda e, m=m, pB=pB: e.matmul(
                pB[:, 0:tile], wb[:, m * 128:(m + 1) * 128], a16[0:ds, r16["bctB"]:r16["bctB"] + tile],
                start=True, stop=True), inc=tS)
            b_mm = tS.n
            op("t", lambda e, m=m, pC=pC: e.matmul(
                pC[:, 0:tile], wc[:, m * 128:(m + 1) * 128], a16[0:ds, r16["bctC"]:r16["bctC"] + tile],
                start=True, stop=True), inc=tS)
            c_mm = tS.n
            szs = A16("sz", (m % 2) * tile, tile)
            wait("s", tS, z_mm)
            silu_done[m] = op("s", lambda e, szs=szs, pz=pz: e.activation(
                szs, pz[:, 0:tile], AF.Silu), inc=sS)
            wait("s", tS, b_mm)
            bev[m] = op("s", lambda e, pB=pB, m=m: e.copy(A16("bbf", (m % 2) * tile, tile), pB[:, 0:tile]), inc=sS)
            wait("s", tS, c_mm)
            cev[m] = op("s", lambda e, pC=pC, m=m: e.copy(A16("cbf", (m % 2) * tile, tile), pC[:, 0:tile]), inc=sS)
            bxs = A16("bx", 0, tile)
            wait("v", sS, bev[m])
            op("v", lambda e, bxs=bxs, m=m: e.tensor_tensor(
                bxs, A16("bbf", (m % 2) * tile, tile), A16("xiT", m * tile, tile), ALU.mult), inc=vS)
            wait("v", vS)
            hts = A32("hT", (m % 2) * tile, tile)
            op("v", lambda e, hts=hts, bxs=bxs, m=m: e.tensor_tensor_scan(
                hts, A32("abc", 0, tile), bxs, scst[:, m:m + 1], ALU.mult, ALU.add), inc=vS)
            wait("v", vS)
            op("v", lambda e, hts=hts, m=m: e.tensor_copy(
                scst[:, m:m + 1], hts[:, tile - 1:tile]), inc=vS)
            wait("v", vS)
            if m >= 4:
                wait("v", gs[(m // 2) % 2])
            wait("v", sS, cev[m])
            op("v", lambda e, hts=hts, m=m: e.tensor_tensor(
                A16("scrA", 0, tile), A16("cbf", (m % 2) * tile, tile), hts, ALU.mult), inc=vS)
            op("v", lambda e, m=m: e.scalar_tensor_tensor(
                A16("scrB", 0, tile), A16("xiT", m * tile, tile), dcol[:, m:m + 1],
                A16("scrA", 0, tile), ALU.mult, ALU.add), inc=vS)
            yls = A16("ylw", (m % 4) * tile, tile)
            op("v", lambda e, yls=yls, szs=szs: e.tensor_tensor(
                yls, A16("scrB", 0, tile), szs, ALU.mult), inc=vS)
            op("v", lambda e, szs=szs, m=m: e.tensor_tensor(
                A16("scrC", 0, tile), A16("cbf", (m % 2) * tile, tile), szs, ALU.mult), inc=vS)
            kws = A16("kw", (m % 4) * tile, tile)
            ylk_done[m] = op("v", lambda e, kws=kws: e.tensor_tensor(
                kws, A16("scrC", 0, tile), A16("qb", 0, tile), ALU.mult), inc=vS)
            pool_done[m] = gP.n
            if m % 2 == 1:
                par = (m // 2) % 2
                wait("g", vS)
                op("g", lambda e, t=t, m=m, par=par: e.dma_start(
                    out=sp_yl[t, :, (m - 1) * tile:(m + 1) * tile],
                    in_=A16("ylw", ((m - 1) % 4) * tile, 2 * tile)), inc=gs[par], k=16)
                op("g", lambda e, t=t, m=m, par=par: e.dma_start(
                    out=sp_k[t, :, (m - 1) * tile:(m + 1) * tile],
                    in_=A16("kw", ((m - 1) % 4) * tile, 2 * tile)), inc=gs[par], k=16)
            stride = max(1, dsl // tsn)
            if cfg.get("noilv"):
                pass
            elif t + 1 < ntiles and m % stride == stride - 1 and m // stride < tsn:
                j = m // stride
                if j + 1 < tsn:
                    LN_load(t + 1, j + 1)
                LN_stats(t + 1, j)
        if cfg.get("noilv") and t + 1 < ntiles:
            for ts_ in range(tsn):
                LN_load(t + 1, ts_)
                LN_stats(t + 1, ts_)
        p1_zmm_t[t] = z_mm
        p1_vend[t] = vS.n
    p1_t = tS.n

    op("v", lambda e: e.tensor_copy(ccsb[:, 0:dsl], scst[:]), inc=vS)
    wait("t", vS)
    op("t", lambda e: e.matmul(p[6][:, 0:1], onesr[:], lsst[:], start=True, stop=True), inc=tS)
    wait("s", tS)
    op("s", lambda e: e.copy(ccsb[:, dsl:dsl + 1], p[6][:, 0:1]), inc=sS)
    wait("g", sS)
    wait("g", vS)
    wait("g", gs[0])
    wait("g", gs[1])
    op("g", lambda e: e.dma_start(out=cc_in_t.ap(), in_=ccsb[:]), inc=gcc, k=16)
    wait("g", gcc)
    op("g", lambda e: e.collective_compute(
        "AllGather", ALU.bypass, replica_groups=[list(range(ncores))],
        ins=[cc_in_t.ap().opt()], outs=[cc_out_t.ap().opt()]), inc=cS)
    wait("g", cS)
    op("g", lambda e: e.dma_start(
        out=agsb[:], in_=cc_out_t.ap().rearrange("r p c -> p r c")), inc=gcc, k=16)

    wait("v", gcc)
    agv = agsb.rearrange("p (r c) -> p r c", c=dsl + 1)
    for j in range(ncores):
        op("v", lambda e, j=j: e.tensor_tensor(
            ctmp8[:], agv[:, :, dsl], cmaske[:, j * ncores:(j + 1) * ncores],
            ALU.mult), inc=vS)
        wait("v", vS)
        op("v", lambda e: e.tensor_reduce(cred[:], ctmp8[:], AX.X, ALU.add), inc=vS)
        wait("s", vS)
        op("s", lambda e: e.activation(cexp[:], cred[:], AF.Exp), inc=sS)
        wait("v", sS)
        op("v", lambda e, j=j: e.tensor_tensor(
            ccoef[:], cexp[:], cmaskm[:, j:j + 1], ALU.mult), inc=vS)
        wait("v", vS)
        op("v", lambda e, j=j: e.tensor_scalar(
            ctmp16[:], agsb[:, j * (dsl + 1):j * (dsl + 1) + dsl],
            ccoef[:, 0:1], None, ALU.mult), inc=vS)
        wait("v", vS)
        op("v", lambda e: e.tensor_tensor(hin[:], hin[:], ctmp16[:], ALU.add), inc=vS)
        wait("v", vS)
    barrier()

    if stop == "xchg":
        for s_ in (ycn, yx[0], yx[1], yst, gw, gs[0], gs[1], gcc):
            wait("y", s_)
        wait("y", cS, cS.n)
        return B.finalize()

    wait("g", tS, p1_t)
    for k in range(km):
        op("g", lambda e, k=k: e.dma_start(
            out=wi_ap(k), in_=wi_in[:, k * f:(k + 1) * f]), inc=gw, k=16)
    wi_gw = gw.n

    wait("y", gs[0])
    wait("y", gs[1])
    xb_rd = {}
    p2a_tile_v = 0
    p2a_tile_t = 0
    for t in range(ntiles):
        if t:
            wait("y", vS, p2a_tile_v)
            wait("y", tS, p2a_tile_t)
        op("y", lambda e, t=t: e.dma_start(
            out=a16[:, yl_in_off:yl_in_off + dsl * tile], in_=sp_yl[t]), inc=yyk, k=16)
        op("y", lambda e, t=t: e.dma_start(
            out=a16[:, k_in_off:k_in_off + dsl * tile], in_=sp_k[t]), inc=yyk, k=16)
        ymm_par = {}
        lnb_par = {}
        sq_par = {}
        stt_done = {}

        def A_stt(ts, t=t):
            par = ts % 2
            yb0 = y2_off + par * dsl * 128
            row = t * tile + ts * 128
            xb = A32("xb", par * dm, dm)
            if t or ts:
                wait("y", vS, xb_rd.get(par, 0))
            op("y", lambda e, row=row, xb=xb: e.dma_start(out=xb, in_=x_in[row:row + 128, :]), inc=yx[par], k=16)
            if ts < 2:
                wait("v", yyk)
            if ts >= 2:
                wait("v", tS, ymm_par[par])
            stt_done[ts] = []
            for m in range(dsl):
                v_ = op("v", lambda e, m=m, ts=ts, yb0=yb0: e.scalar_tensor_tensor(
                    a16[:, yb0 + m * 128:yb0 + (m + 1) * 128],
                    a16[:, k_in_off + m * tile + ts * 128:k_in_off + m * tile + ts * 128 + 128],
                    hin[:, m:m + 1],
                    a16[:, yl_in_off + m * tile + ts * 128:yl_in_off + m * tile + ts * 128 + 128],
                    ALU.mult, ALU.add), inc=vS)
                stt_done[ts].append(v_)

        def B_mm(ts):
            par = ts % 2
            pp = [p[2 * par + n] for n in range(n512)]
            yb0 = y2_off + par * dsl * 128
            if ts >= 2:
                wait("t", vS, lnb_par[par])
                wait("t", vS, sq_par[par][0])
                if sq_par[par][1]:
                    wait("t", sS, sq_par[par][1])
            for m in range(dsl):
                wait("t", vS, stt_done[ts][m])
                for n in range(n512):
                    op("t", lambda e, m=m, n=n, yb0=yb0, pp=pp: e.matmul(
                        pp[n][:, 0:512],
                        a16[:, yb0 + m * 128:yb0 + (m + 1) * 128],
                        wout_ap(m)[:, n * 512:(n + 1) * 512],
                        start=(m == 0), stop=(m == dsl - 1)),
                        inc=tS if (m == dsl - 1 and n == n512 - 1) else None)
            ymm_par[par] = tS.n

        def C_ln(ts, t=t):
            par = ts % 2
            pp = [p[2 * par + n] for n in range(n512)]
            wo_mm = ymm_par[par]

            def _w3(wo_mm=wo_mm):
                wait("v", tS, wo_mm)
            sq_par[par] = emit_ln_stats([(pp[n][:, 0:512], 512) for n in range(n512)],
                                        pre_wait=_w3)
            wait("v", sS)
            for n in range(n512):
                op("v", lambda e, n=n, pp=pp: e.tensor_scalar(
                    A32("lnb", n * 512, 512), pp[n][:, 0:512], A32("stat", 4, 1), A32("stat", 9, 1),
                    ALU.subtract, ALU.mult), inc=vS)
            lnb_par[par] = vS.n
            xb = A32("xb", par * dm, dm)
            wait("v", yx[par])
            wait("v", yst)
            op("v", lambda e, xb=xb: e.tensor_tensor(
                A32("fout", 0, dm), xb, A32("lnb", 0, dm), ALU.add), inc=vS)
            xb_rd[par] = vS.n
            wait("y", vS)
            op("y", lambda e, t=t, ts=ts: e.dma_start(
                out=sp_x1.ap()[t * tsn + ts], in_=A32("fout", 0, dm)), inc=yst, k=16)

        A_stt(0)
        if tsn > 1:
            A_stt(1)
        for ts in range(tsn):
            B_mm(ts)
            C_ln(ts)
            if ts + 2 < tsn:
                A_stt(ts + 2)
        p2a_tile_v = vS.n
        p2a_tile_t = tS.n
    p2a_yst = yst.n
    p2a_t = tS.n
    if stop == "p2a":
        for s_ in (ycn, yx[0], yx[1], yyk, yst, gw, gs[0], gs[1], gcc):
            wait("y", s_)
        wait("y", cS, cS.n)
        return B.finalize()

    wait("g", tS, p2a_t)
    for k in range(wo_in_b):
        op("g", lambda e, k=k: e.dma_start(
            out=wo_ap(k), in_=wo_in[:, k * dm:(k + 1) * dm]), inc=gw, k=16)

    wait("t", gw, wi_gw)
    wait("y", yst, p2a_yst)
    wi_x1_rd = 0
    wi_lnf_t = {}
    for t in range(ntiles):
        lnf0 = r16["ln0T"] if t % 2 == 0 else r16["ylw"]
        um0 = (t % 2) * tile
        if t:
            wait("y", vS, wi_x1_rd)
        op("y", lambda e, t=t: e.dma_start(
            out=A32("x1", 0, tsn * dm).rearrange("p (r c) -> p r c", c=dm),
            in_=sp_x1.ap()[t * tsn:(t + 1) * tsn].rearrange("r p c -> p r c")), inc=yxl, k=16)
        if t >= 2:
            wait("t", tS, wi_lnf_t[t % 2])
        for ts in range(tsn):
            x1b = A32("x1", ts * dm, dm)
            def _w2():
                wait("v", yxl)
            emit_ln_stats([(x1b, dm)], pre_wait=_w2)
            wait("v", sS)
            wait("v", yxl)
            op("v", lambda e, x1b=x1b: e.tensor_scalar(
                A16("lntok", 0, dm), x1b, A32("stat", 4, 1), A32("stat", 9, 1),
                ALU.subtract, ALU.mult), inc=vS)
            wait("t", vS)
            for g0 in range(0, km, 4):
                gn = min(4, km - g0)
                for j in range(gn):
                    op("t", lambda e, j=j, g0=g0: e.transpose(
                        pT[:, j * 128:(j + 1) * 128],
                        A16("lntok", (g0 + j) * 128, 128), idb[:]),
                        inc=tS if j == gn - 1 else None)
                wait("s", tS)

                def evac_lnf(e, g0=g0, gn=gn, ts=ts, lnf0=lnf0):
                    dst = a16[:, lnf0 + g0 * tile:
                              lnf0 + (g0 + gn) * tile].rearrange(
                        "p (k n) -> p k n", n=tile)[:, :, ts * 128:(ts + 1) * 128]
                    src = pT.rearrange("p (k n) -> p k n", n=128)[:, 0:gn, :]
                    return e.tensor_copy(dst, src)
                wait("v", tS)
                op("v", evac_lnf, inc=vS)
                wait("t", vS)
            wait("t", vS)
            for k in range(km):
                op("t", lambda e, k=k, ts=ts, lnf0=lnf0: e.matmul(
                    p[2][:, 0:e_], a16[:, lnf0 + k * tile + ts * 128:lnf0 + k * tile + ts * 128 + 128],
                    gatet[:, k * e_:(k + 1) * e_],
                    start=(k == 0), stop=(k == km - 1)),
                    inc=tS if k == km - 1 else None)
            wait("v", tS)
            op("v", lambda e: e.tensor_tensor(lg8[:], p[2][:, 0:e_], gateb[:], ALU.add), inc=vS)
            wait("v", vS)
            op("v", lambda e: e.tensor_reduce(mx1[:], lg8[:], AX.X, ALU.max), inc=vS)
            wait("v", vS)
            op("v", lambda e: e.tensor_scalar(pr8[:], lg8[:], mx1[:, 0:1], None, ALU.subtract), inc=vS)
            wait("v", vS)
            wait("s", vS)
            op("s", lambda e: e.activation(eq8[:], pr8[:], AF.Exp, scale=BIG), inc=sS)
            wait("v", sS)
            op("v", lambda e: e.tensor_tensor(pr8[:], eq8[:], iotar[:], ALU.mult), inc=vS)
            wait("v", vS)
            op("v", lambda e: e.tensor_reduce(mx1[:], pr8[:], AX.X, ALU.max), inc=vS)
            wait("v", vS)
            op("v", lambda e: e.tensor_scalar(pr8b[:], pr8[:], mx1[:, 0:1], None, ALU.subtract), inc=vS)
            wait("s", vS)
            op("s", lambda e: e.activation(oh8[:], pr8b[:], AF.Exp, scale=BIG), inc=sS)
            wait("v", sS)
            oh_v = vS.n
            if dbg:
                wait("y", vS)
                op("y", lambda e, t=t, ts=ts: e.dma_start(
                    out=sp_dbg.ap()[t, ts, :, 0:e_], in_=lg8[:]), inc=yst, k=16)
                op("y", lambda e, t=t, ts=ts: e.dma_start(
                    out=sp_dbg.ap()[t, ts, :, e_:2 * e_], in_=eq8[:]), inc=yst, k=16)
                op("y", lambda e, t=t, ts=ts: e.dma_start(
                    out=sp_dbg.ap()[t, ts, :, 2 * e_:3 * e_], in_=oh8[:]), inc=yst, k=16)
                op("y", lambda e, t=t, ts=ts: e.dma_start(
                    out=sp_dbg.ap()[t, ts, :, 3 * e_:3 * e_ + 1], in_=mx1[:, 0:1],
                    allow_slow_non_contiguous=True), inc=yst, k=16)
            wait("t", vS)
            for k in range(km):
                op("t", lambda e, k=k, ts=ts, lnf0=lnf0: e.matmul(
                    p[3][:, 0:er], a16[:, lnf0 + k * tile + ts * 128:lnf0 + k * tile + ts * 128 + 128],
                    lorat[:, k * er:(k + 1) * er],
                    start=(k == 0), stop=(k == km - 1)),
                    inc=tS if k == km - 1 else None)
            wait("v", tS)
            wait("v", vS, oh_v)
            for i in range(e_):
                op("v", lambda e, i=i: e.tensor_scalar(
                    umb[:, i * r_:(i + 1) * r_], p[3][:, i * r_:(i + 1) * r_],
                    oh8[:, i:i + 1], None, ALU.mult), inc=vS)
            wait("t", vS)
            op("t", lambda e: e.transpose(pT[0:er, 0:128], umb[:], idb[:]), inc=tS)
            wait("v", tS)
            op("v", lambda e, ts=ts, um0=um0: e.tensor_copy(umt[:, um0 + ts * 128:um0 + (ts + 1) * 128], pT[0:er, 0:128]), inc=vS)
            wait("t", vS)
        lnf_done_v = vS.n
        wi_x1_rd = vS.n
        wait("t", vS, lnf_done_v)
        relu_done = {}
        for mfm in range(fm):
            pb = p[mfm % 2]
            if mfm >= 2:
                wait("t", sS, relu_done[mfm - 2])
            for k in range(km):
                op("t", lambda e, k=k, mfm=mfm, pb=pb, lnf0=lnf0: e.matmul(
                    pb[:, 0:tile], wi_ap(k)[:, mfm * 128:(mfm + 1) * 128],
                    a16[:, lnf0 + k * tile:lnf0 + (k + 1) * tile],
                    start=(k == 0), stop=False))
            op("t", lambda e, mfm=mfm, pb=pb, um0=um0: e.matmul(
                pb[:, 0:tile], lorab[:, mfm * 128:(mfm + 1) * 128], umt[:, um0:um0 + tile],
                start=False, stop=True), inc=tS)
            wait("s", tS)
            if mfm >= 2:
                wait("s", gi[mfm % 2])
            ists = A16("ist", (mfm % 2) * tile, tile)
            relu_done[mfm] = op("s", lambda e, ists=ists, mfm=mfm, pb=pb: e.activation(
                ists, pb[:, 0:tile], AF.Relu, bias=wibcol[:, mfm:mfm + 1]), inc=sS)
            wait("g", sS)
            op("g", lambda e, t=t, mfm=mfm, ists=ists: e.dma_start(
                out=sp_int.ap()[t, :, mfm, :], in_=ists), inc=gi[mfm % 2], k=16)
        wi_lnf_t[t % 2] = tS.n
    p2bwi_t = tS.n

    wait("g", tS, p2bwi_t)
    for k in range(wo_in_b, fm):
        op("g", lambda e, k=k: e.dma_start(
            out=wo_ap(k), in_=wo_in[:, k * dm:(k + 1) * dm]), inc=gw, k=16)

    wait("t", gw)
    wait("y", gi[0])
    wait("y", gi[1])
    wo_tile_t = 0
    wo_tile_v = 0
    for t in range(ntiles):
        if t:
            wait("y", tS, wo_tile_t)
            wait("y", vS, wo_tile_v)
        op("y", lambda e, t=t: e.dma_start(
            out=a16[:, 0:fm * tile],
            in_=sp_int.ap()[t].rearrange("p k n -> p (k n)")), inc=yil, k=16)
        op("y", lambda e, t=t: e.dma_start(
            out=A32("x1", 0, tsn * dm).rearrange("p (r c) -> p r c", c=dm),
            in_=sp_x1.ap()[t * tsn:(t + 1) * tsn].rearrange("r p c -> p r c")), inc=yxl, k=16)
        fo_par = {}
        for ts in range(tsn):
            par = ts % 2
            pp = [p[2 * par + n] for n in range(n512)]
            wait("t", yil)
            wait("v", yxl)
            if ts >= 2:
                wait("t", vS, fo_par[par])
            for n in range(n512):
                for k in range(fm):
                    op("t", lambda e, n=n, k=k, ts=ts, pp=pp: e.matmul(
                        pp[n][:, 0:512],
                        a16[:, k * tile + ts * 128:k * tile + ts * 128 + 128],
                        wo_ap(k)[:, n * 512:(n + 1) * 512],
                        start=(k == 0), stop=(k == fm - 1)),
                        inc=tS if k == fm - 1 else None)
            wo_mm = tS.n
            wait("v", tS, wo_mm)
            for n in range(n512):
                op("v", lambda e, n=n, pp=pp: e.tensor_tensor(A32("fout", n * 512, 512), pp[n][:, 0:512],
                                                              wobb[:, n * 512:(n + 1) * 512], ALU.add), inc=vS)
            fo_par[par] = vS.n
            emit_ln_stats([(A32("fout", 0, dm), dm)])
            wait("v", sS)
            ob = A32("outb", 0, dm)
            op("v", lambda e: e.tensor_scalar(
                A32("lnb", 0, dm), A32("fout", 0, dm), A32("stat", 4, 1), A32("stat", 9, 1),
                ALU.subtract, ALU.mult), inc=vS)
            wait("v", yst)
            op("v", lambda e, ob=ob, ts=ts: e.tensor_tensor(
                ob, A32("x1", ts * dm, dm), A32("lnb", 0, dm), ALU.add), inc=vS)
            wait("y", vS)
            row = t * tile + ts * 128
            op("y", lambda e, row=row, ob=ob: e.dma_start(
                out=o_out[row:row + 128, :], in_=ob), inc=yst, k=16)
            if ts == tsn - 1:
                wo_tile_t = tS.n
                wo_tile_v = vS.n

    for s_ in (ycn, yx[0], yx[1], yyk, yxl, yil, yst, gw, gs[0], gs[1], gi[0], gi[1], gcc):
        wait("y", s_)
    wait("y", cS, cS.n)
    return B.finalize()


def make_core_inputs(cfg, inp, core, shared_cache):
    dm, di, ds, f, e_, r_ = (cfg[k] for k in ["dm", "di", "ds", "f", "e", "r"])
    nt, ncores, nbatch = cfg["nt"], cfg["ncores"], cfg["nbatch"]
    km, dsl, fm, er = dm // 128, di // 128, f // 128, e_ * r_
    chain = ncores // nbatch

    C = np.ascontiguousarray

    def ktile(w, kdim):
        n = w.shape[1]
        return C(w.reshape(kdim, 128, n).transpose(1, 0, 2).reshape(128, kdim * n))

    if not shared_cache:
        shared_cache["x"] = np.asarray(inp["L0_attn_x"], np.float32).reshape(ncores, nt, dm)
        W_in = np.asarray(inp["L0_attn_W_in"], np.float32)
        W_xp = np.asarray(inp["L0_attn_W_xp"], np.float32)
        W_out = np.asarray(inp["L0_attn_W_out"], np.float32)
        wi_W = np.asarray(inp["L0_ffn_wi_W"], np.float32)
        wo_W = np.asarray(inp["L0_ffn_wo_W"], np.float32)
        gate_W = np.asarray(inp["L0_ffn_gate_W"], np.float32)
        lora_A = np.asarray(inp["L0_ffn_lora_A"], np.float32)
        lora_B = np.asarray(inp["L0_ffn_lora_B"], np.float32)
        D = np.asarray(inp["L0_attn_D"], np.float32)
        wi_b = np.asarray(inp["L0_ffn_wi_b"], np.float32)
        wo_b = np.asarray(inp["L0_ffn_wo_b"], np.float32)
        gate_b = np.asarray(inp["L0_ffn_gate_b"], np.float32)
        shared_cache["w"] = {
            "w_win": ktile(W_in, km),
            "w_wxp": ktile(W_xp, dsl),
            "w_wb": C(np.asarray(inp["L0_attn_W_B"], np.float32)),
            "w_wc": C(np.asarray(inp["L0_attn_W_C"], np.float32)),
            "w_wout": ktile(W_out, dsl),
            "w_wi": ktile(C(wi_W.T), km),
            "w_wo": ktile(C(wo_W.T), fm),
            "w_gatet": ktile(C(gate_W.T), km),
            "w_lorat": ktile(C(lora_A.reshape(er, dm).T), km),
            "w_lorab": C(lora_B.transpose(0, 2, 1).reshape(er, f)),
            "w_dcol": C(D.reshape(dsl, 128).T),
            "w_wibcol": C(wi_b.reshape(fm, 128).T),
            "w_wobb": C(np.broadcast_to(wo_b, (128, dm))),
            "w_gateb": C(np.broadcast_to(gate_b, (128, e_))),
            "w_iotar": C(np.broadcast_to((e_ - np.arange(e_, dtype=np.float32)).reshape(1, e_), (128, e_))),
            "w_alog": np.asarray(inp["L0_attn_A_log"], np.float32).reshape(1, 1),
            "w_ident": np.eye(128, dtype=np.float32),
            "w_onesr": np.ones((1, 128), np.float32),
        }
    b0 = (core // chain) * chain
    me = np.zeros((ncores, ncores), np.float32)
    mmv = np.zeros((ncores,), np.float32)
    for j in range(b0, core):
        mmv[j] = 1.0
        for k in range(j + 1, core):
            me[j, k] = 1.0
    d = dict(shared_cache["w"])
    d["x"] = C(shared_cache["x"][core])
    d["w_cmaske"] = C(np.broadcast_to(me.reshape(1, -1), (128, ncores * ncores)))
    d["w_cmaskm"] = C(np.broadcast_to(mmv.reshape(1, -1), (128, ncores)))
    return d


_PROGRAM_CACHE = {}


def kernel(**inputs):
    cfg = default_cfg()
    if "main" not in _PROGRAM_CACHE:
        _PROGRAM_CACHE["main"] = build_program(cfg)
    nc = _PROGRAM_CACHE["main"]
    ncores = cfg["ncores"]
    shared = {}
    in_maps = [make_core_inputs(cfg, inputs, c, shared) for c in range(ncores)]
    res = run_bass_kernel_spmd(nc, in_maps, core_ids=list(range(ncores)))
    nt, dm = cfg["nt"], cfg["dm"]
    full = np.empty((ncores, nt, dm), np.float32)
    for c in range(ncores):
        full[c] = res.results[c]["o"].reshape(nt, dm)
    return full.reshape(cfg["nbatch"], (ncores // cfg["nbatch"]) * nt, dm)


# version 22
# speedup vs baseline: 1.1286x; 1.1286x over previous
import numpy as np

import concourse.bass as bass
import concourse.mybir as mybir
from concourse.bass_utils import run_bass_kernel_spmd

F32 = mybir.dt.float32
BF16 = mybir.dt.bfloat16
AF = mybir.ActivationFunctionType
ALU = mybir.AluOpType
AX = mybir.AxisListType

BIG = 1.0e30


def default_cfg():
    return dict(dm=1024, di=2048, ds=64, f=4096, e=8, r=4,
                nt=4096, tile=512, ncores=8, nbatch=2)


class Sem:
    def __init__(self, h):
        self.h = h
        self.n = 0


class Builder:

    def __init__(self, racecheck=False):
        self.nc = bass.Bass("TRN2", target_bir_lowering=False,
                            detect_race_conditions=racecheck)
        self.q = {"t": [], "v": [], "s": [], "y": [], "g": []}
        self._ctx = []

    def ent(self, t):
        self._ctx.append(t)
        return t.__enter__()

    def sb(self, nm, shape, dt):
        return self.ent(self.nc.sbuf_tensor(nm, shape, dt))

    def ps(self, nm, shape, dt=F32):
        return self.ent(self.nc.psum_tensor(nm, shape, dt))

    def sem(self, nm):
        return Sem(self.ent(self.nc.semaphore(nm)))

    def op(self, eng, fn, inc=None, k=1):
        if inc is None:
            self.q[eng].append(fn)
            return None
        inc.n += k
        self.q[eng].append(lambda e, f=fn, s=inc.h, kk=k: f(e).then_inc(s, kk))
        return inc.n

    def wait(self, eng, sem, val=None):
        v = sem.n if val is None else val
        if v <= 0:
            return
        self.q[eng].append(lambda e, s=sem.h, vv=v: e.wait_ge(s, vv))

    def finalize(self):
        nc = self.nc
        q = self.q
        with nc.Block() as block:
            @block.tensor
            def _(e):
                for f in q["t"]:
                    f(e)

            @block.vector
            def _(e):
                for f in q["v"]:
                    f(e)

            @block.scalar
            def _(e):
                for f in q["s"]:
                    f(e)

            @block.sync
            def _(e):
                for f in q["y"]:
                    f(e)

            @block.gpsimd
            def _(e):
                for f in q["g"]:
                    f(e)
        for t in reversed(self._ctx):
            t.__exit__(None, None, None)
        return nc


def build_program(cfg):
    dm, di, ds, f, e_, r_ = (cfg[k] for k in ["dm", "di", "ds", "f", "e", "r"])
    nt, tile, ncores, nbatch = (cfg[k] for k in ["nt", "tile", "ncores", "nbatch"])
    stop = cfg.get("stop", "all")
    km = dm // 128
    dsl = di // 128
    fm = f // 128
    er = e_ * r_
    pw = 2 * ds + 1
    ntiles = nt // tile
    tsn = tile // 128
    nrow = nt // 128
    n512 = dm // 512

    B = Builder(racecheck=cfg.get("racecheck", False))
    nc = B.nc
    op, wait = B.op, B.wait

    def din(nm, shape, dt=F32):
        return nc.dram_tensor(nm, shape, dt, kind="ExternalInput").ap()

    x_in = din("x", [nt, dm])
    win_in = din("w_win", [128, km * 2 * di])
    wxp_in = din("w_wxp", [128, dsl * pw])
    wb_in = din("w_wb", [ds, di])
    wc_in = din("w_wc", [ds, di])
    wout_in = din("w_wout", [128, dsl * dm])
    wi_in = din("w_wi", [128, km * f])
    wo_in = din("w_wo", [128, fm * dm])
    gatet_in = din("w_gatet", [128, km * e_])
    lorat_in = din("w_lorat", [128, km * er])
    lorab_in = din("w_lorab", [er, f])
    dcol_in = din("w_dcol", [128, dsl])
    wibcol_in = din("w_wibcol", [128, fm])
    wobb_in = din("w_wobb", [128, dm])
    gateb_in = din("w_gateb", [128, e_])
    alog_in = din("w_alog", [1, 1])
    ident_in = din("w_ident", [128, 128])
    onesr_in = din("w_onesr", [1, 128])
    iotar_in = din("w_iotar", [128, e_])
    cmaske_in = din("w_cmaske", [128, ncores * ncores])
    cmaskm_in = din("w_cmaskm", [128, ncores])

    o_out = nc.dram_tensor("o", [nt, dm], F32, kind="ExternalOutput").ap()

    dbg = cfg.get("dbg", False)
    dbgkw = dict(kind="ExternalOutput") if dbg else {}
    sp_yl = nc.dram_tensor("sp_yl", [ntiles, 128, dsl * tile], BF16, **dbgkw).ap()
    sp_k = nc.dram_tensor("sp_k", [ntiles, 128, dsl * tile], BF16, **dbgkw).ap()
    sp_x1 = nc.dram_tensor("sp_x1", [nrow, 128, dm], F32, **dbgkw)
    sp_int = nc.dram_tensor("sp_int", [ntiles, 128, fm, tile], BF16, **dbgkw)
    sp_dbg = nc.dram_tensor("sp_dbg", [ntiles, tsn, 128, 4 * e_], F32,
                            **dbgkw) if dbg else None
    cc_in_t = nc.dram_tensor("cc_in", [128, dsl + 1], F32)
    cc_out_t = nc.dram_tensor("cc_out", [ncores, 128, dsl + 1], F32)

    acols = max(km * 2 * di, km * f)
    bcols = dsl * dm
    wo_in_b = min(fm, bcols // dm)
    wbig = B.sb("wbig", [128, acols + bcols], BF16)

    def win_ap(k):
        return wbig[:, k * 2 * di:(k + 1) * 2 * di]

    def wout_ap(k):
        return wbig[:, acols + k * dm:acols + (k + 1) * dm]

    def wi_ap(k):
        return wbig[:, k * f:(k + 1) * f]

    def wo_ap(k):
        if k < wo_in_b:
            return wbig[:, acols + k * dm:acols + (k + 1) * dm]
        kk = k - wo_in_b
        return wbig[:, kk * dm:(kk + 1) * dm]

    wxp = B.sb("wxp", [128, dsl * pw], BF16)
    wb = B.sb("wb", [ds, di], BF16)
    wc = B.sb("wc", [ds, di], BF16)
    lorab = B.sb("lorab", [er, f], BF16)
    lorat = B.sb("lorat", [128, km * er], BF16)
    gatet = B.sb("gatet", [128, km * e_], BF16)
    dcol = B.sb("dcol", [128, dsl], F32)
    wibcol = B.sb("wibcol", [128, fm], F32)
    wobb = B.sb("wobb", [128, dm], BF16)
    gateb = B.sb("gateb", [128, e_], F32)
    idb = B.sb("idb", [128, 128], BF16)
    idf = B.sb("idf", [128, 128], F32)
    onesr = B.sb("onesr", [1, 128], F32)
    onest = B.sb("onest", [1, tile], F32)
    alog_sb = B.sb("alog_sb", [1, 1], F32)
    acol = B.sb("acol", [128, 1], F32)
    iotar = B.sb("iotar", [128, e_], F32)
    cmaske = B.sb("cmaske", [128, ncores * ncores], F32)
    cmaskm = B.sb("cmaskm", [128, ncores], F32)
    ccsb = B.sb("ccsb", [128, dsl + 1], F32)
    agsb = B.sb("agsb", [128, ncores * (dsl + 1)], F32)
    hin = B.sb("hin", [128, dsl], F32)
    scst = B.sb("scst", [128, dsl], F32)
    lsst = B.sb("lsst", [1, 1], F32)
    larow = B.sb("larow", [1, tile], F32)
    lsrow = B.sb("lsrow", [1, tile], F32)
    umt = B.sb("umt", [er, 2 * tile], BF16)
    umb = B.sb("umb", [128, er], BF16)
    oh8 = B.sb("oh8", [128, e_], F32)
    eq8 = B.sb("eq8", [128, e_], F32)
    pr8 = B.sb("pr8", [128, e_], F32)
    pr8b = B.sb("pr8b", [128, e_], F32)
    lg8 = B.sb("lg8", [128, e_], F32)
    mx1 = B.sb("mx1", [128, 1], F32)
    cred = B.sb("cred", [128, 1], F32)
    cexp = B.sb("cexp", [128, 1], F32)
    ccoef = B.sb("ccoef", [128, 1], F32)
    ctmp8 = B.sb("ctmp8", [128, ncores], F32)
    ctmp16 = B.sb("ctmp16", [128, dsl], F32)

    r16 = {}
    off = 0
    for nm, sz in [("xiT", dsl * tile), ("ln0T", km * tile), ("lntok", dm),
                   ("bctB", tile), ("bctC", tile), ("bx", tile),
                   ("sz", 2 * tile), ("ylw", 4 * tile), ("kw", 4 * tile),
                   ("qb", tile), ("scrA", tile), ("scrB", tile), ("scrC", tile),
                   ("bbf", 2 * tile), ("cbf", 2 * tile)]:
        r16[nm] = off
        off += sz
    r16["ist"] = r16["scrA"]
    a16cols = max(off, fm * tile, 2 * dsl * tile + 2 * dsl * 128)
    a16 = B.sb("a16", [128, a16cols], BF16)

    def A16(nm, lo, sz):
        return a16[:, r16[nm] + lo:r16[nm] + lo + sz]

    yl_in_off = 0
    k_in_off = dsl * tile
    y2_off = 2 * dsl * tile

    r32 = {}
    off = 0
    for nm, sz in [("stat", 26), ("proj", pw + 3), ("fout", dm), ("lnb", dm),
                   ("U", max(tile + 2 * tile + 2 * dm, tsn * dm + dm))]:
        r32[nm] = off
        off += sz
    r32["sq"] = r32["lnb"]
    r32["abc"] = r32["U"]
    r32["hT"] = r32["U"] + tile
    r32["xb"] = r32["U"] + 3 * tile
    r32["x1"] = r32["U"]
    r32["outb"] = r32["U"] + tsn * dm
    a32 = B.sb("a32", [128, off], F32)

    def A32(nm, lo, sz):
        return a32[:, r32[nm] + lo:r32[nm] + lo + sz]

    p = [B.ps("p%d" % i, [128, 512]) for i in range(7)]
    pT = B.ps("pT", [128, 512], BF16)
    epsc = B.sb("epsc", [128, 1], F32)

    ycn = B.sem("ycn")
    yx = [B.sem("yx0"), B.sem("yx1")]
    yyk = B.sem("yyk")
    yxl = B.sem("yxl")
    yil = B.sem("yil")
    yst = B.sem("yst")
    gw = B.sem("gw")
    gs = [B.sem("gs0"), B.sem("gs1")]
    gi = [B.sem("gi0"), B.sem("gi1")]
    gcc = B.sem("gcc")
    gP = B.sem("gP")
    tS = B.sem("tS")
    vS = B.sem("vS")
    sS = B.sem("sS")
    cS = B.sem("cS")


    def emit_ln_stats(srcs, pre_wait=None, m_slot=4, r_slot=9, defer_rstd=False):
        if pre_wait is not None:
            pre_wait()
        is_psum = bool(srcs and getattr(srcs[0][0].tensor, "name", "").startswith("p"))
        for i, (ap_, w_) in enumerate(srcs):
            op("v", lambda e, ap_=ap_, i=i: e.tensor_reduce(
                A32("stat", 0 + i, 1), ap_, AX.X, ALU.add), inc=vS)
        if is_psum:
            wait("s", vS)
            for i, (ap_, w_) in enumerate(srcs):
                op("s", lambda e, ap_=ap_, w_=w_, i=i: e.activation(
                    A32("sq", 0, w_), ap_, AF.Square,
                    accum_out=A32("stat", 2 + i, 1)), inc=sS)
            reads_done = (vS.n, sS.n)
            wait("v", sS)
        else:
            for i, (ap_, w_) in enumerate(srcs):
                op("v", lambda e, ap_=ap_, w_=w_, i=i: e.tensor_tensor(
                    A32("sq", 0, w_), ap_, ap_, ALU.mult), inc=vS)
                op("v", lambda e, w_=w_, i=i: e.tensor_reduce(
                    A32("stat", 2 + i, 1), A32("sq", 0, w_), AX.X, ALU.add), inc=vS)
            reads_done = (vS.n, 0)
        wait("v", vS)
        if len(srcs) == 2:
            op("v", lambda e: e.tensor_tensor(A32("stat", 0, 1), A32("stat", 0, 1), A32("stat", 1, 1), ALU.add), inc=vS)
            op("v", lambda e: e.tensor_tensor(A32("stat", 2, 1), A32("stat", 2, 1), A32("stat", 3, 1), ALU.add), inc=vS)
            wait("v", vS)
        op("v", lambda e, m_slot=m_slot: e.tensor_scalar(A32("stat", m_slot, 1), A32("stat", 0, 1), 1.0 / dm, None, ALU.mult), inc=vS)
        op("v", lambda e: e.tensor_scalar(A32("stat", 5, 1), A32("stat", 2, 1), 1.0 / dm, None, ALU.mult), inc=vS)
        wait("v", vS)
        op("v", lambda e, m_slot=m_slot: e.tensor_tensor(A32("stat", 6, 1), A32("stat", m_slot, 1), A32("stat", m_slot, 1), ALU.mult), inc=vS)
        wait("v", vS)
        if defer_rstd:
            op("v", lambda e, r_slot=r_slot: e.tensor_scalar(
                A32("stat", r_slot, 1), A32("stat", 5, 1), A32("stat", 6, 1), 1e-5,
                ALU.subtract, ALU.add), inc=vS)
            return reads_done
        op("v", lambda e: e.tensor_scalar(A32("stat", 7, 1), A32("stat", 5, 1), A32("stat", 6, 1), 1e-5,
                                          ALU.subtract, ALU.add), inc=vS)
        wait("s", vS)
        op("s", lambda e: e.activation(A32("stat", 7, 1), A32("stat", 7, 1), AF.Ln), inc=sS)
        wait("s", sS)
        op("s", lambda e, r_slot=r_slot: e.activation(A32("stat", r_slot, 1), A32("stat", 7, 1), AF.Exp, scale=-0.5), inc=sS)
        return reads_done

    def barrier():
        tv, vv, sv = tS.n, vS.n, sS.n
        for eng in ("t", "v", "s", "y", "g"):
            wait(eng, tS, tv)
            wait(eng, vS, vv)
            wait(eng, sS, sv)

    for nm, dst, src in [("idf", idf, ident_in), ("onesr", onesr, onesr_in),
                         ("dcol", dcol, dcol_in), ("wibcol", wibcol, wibcol_in),
                         ("gateb", gateb, gateb_in), ("iotar", iotar, iotar_in),
                         ("alog", alog_sb, alog_in), ("cmaske", cmaske, cmaske_in),
                         ("cmaskm", cmaskm, cmaskm_in)]:
        op("y", lambda e, d=dst, s=src: e.dma_start(out=d[:], in_=s), inc=ycn, k=16)
    op("g", lambda e: e.dma_start(out=wobb[:], in_=wobb_in), inc=gw, k=16)
    op("g", lambda e: e.dma_start(out=wbig[:, 0:km * 2 * di], in_=win_in), inc=gw, k=16)
    op("g", lambda e: e.dma_start(out=wxp[:], in_=wxp_in), inc=gw, k=16)
    op("g", lambda e: e.dma_start(out=wb[:], in_=wb_in), inc=gw, k=16)
    op("g", lambda e: e.dma_start(out=wc[:], in_=wc_in), inc=gw, k=16)
    op("g", lambda e: e.dma_start(out=idb[:], in_=ident_in), inc=gw, k=16)
    op("g", lambda e: e.dma_start(out=gatet[:], in_=gatet_in), inc=gw, k=16)
    op("g", lambda e: e.dma_start(out=lorat[:], in_=lorat_in), inc=gw, k=16)
    op("g", lambda e: e.dma_start(out=lorab[:], in_=lorab_in), inc=gw, k=16)
    for k in range(dsl):
        op("g", lambda e, k=k: e.dma_start(
            out=wout_ap(k), in_=wout_in[:, k * dm:(k + 1) * dm]), inc=gw, k=16)

    wait("s", ycn)
    op("s", lambda e: e.activation(alog_sb[:], alog_sb[:], AF.Exp), inc=sS)
    wait("v", sS)
    op("v", lambda e: e.tensor_scalar(alog_sb[:], alog_sb[:], -1.0, None, ALU.mult), inc=vS)
    op("v", lambda e: e.memset(onest[:], 1.0), inc=vS)
    op("v", lambda e: e.memset(epsc[:], 1e-5), inc=vS)
    op("v", lambda e: e.memset(scst[:], 0.0), inc=vS)
    op("v", lambda e: e.memset(lsst[:], 0.0), inc=vS)
    op("v", lambda e: e.memset(hin[:], 0.0), inc=vS)
    wait("t", vS)
    op("t", lambda e: e.matmul(p[6][:, 0:1], onesr[:], alog_sb[:], start=True, stop=True), inc=tS)
    wait("s", tS)
    op("s", lambda e: e.copy(acol[:], p[6][:, 0:1]), inc=sS)

    wait("t", gw)
    wait("v", gw)
    wait("s", gw)
    wait("t", ycn)
    wait("v", ycn)
    wait("s", ycn)
    barrier()

    xb_guard = {}
    ln_stat_v = {}
    p1_zmm_t = {}
    p1_vend = {}

    def LN_load(t, ts):
        row = t * tile + ts * 128
        xb = A32("xb", (ts % 2) * dm, dm)
        g = xb_guard.get(ts % 2)
        if g:
            wait("y", vS, g)
        op("y", lambda e, row=row, xb=xb: e.dma_start(
            out=xb, in_=x_in[row:row + 128, :]), inc=yx[ts % 2], k=16)

    def LN_stats(t, ts):
        xb = A32("xb", (ts % 2) * dm, dm)

        def _w(ts=ts):
            wait("v", yx[ts % 2])
        emit_ln_stats([(xb, dm)], pre_wait=_w, m_slot=16 + 2 * ts, r_slot=17 + 2 * ts,
                      defer_rstd=True)
        xb_guard[ts % 2] = vS.n
        ln_stat_v[(t, ts)] = (vS.n, sS.n)

    def LN_finish(t):
        for ts in range(tsn):
            wait("s", vS, ln_stat_v[(t, ts)][0])
            op("s", lambda e, ts=ts: e.activation(A32("stat", 17 + 2 * ts, 1),
                                                  A32("stat", 17 + 2 * ts, 1), AF.Ln), inc=sS)
            wait("s", sS)
            op("s", lambda e, ts=ts: e.activation(A32("stat", 17 + 2 * ts, 1),
                                                  A32("stat", 17 + 2 * ts, 1), AF.Exp, scale=-0.5), inc=sS)
        rstd_s = sS.n
        tr_done = 0
        for ts in range(tsn):
            xb = A32("xb", (ts % 2) * dm, dm)
            row = t * tile + ts * 128
            g = xb_guard.get(ts % 2)
            if g:
                wait("y", vS, g)
            op("y", lambda e, row=row, xb=xb: e.dma_start(
                out=xb, in_=x_in[row:row + 128, :]), inc=yx[ts % 2], k=16)
            wait("v", yx[ts % 2])
            wait("v", sS, rstd_s)
            if tr_done:
                wait("v", tS, tr_done)
            op("v", lambda e, xb=xb, ts=ts: e.tensor_scalar(
                A16("lntok", 0, dm), xb, A32("stat", 16 + 2 * ts, 1), A32("stat", 17 + 2 * ts, 1),
                ALU.subtract, ALU.mult), inc=vS)
            xb_guard[ts % 2] = vS.n
            wait("t", vS)
            if t > 0 and ts == 0:
                pass
            for g0 in range(0, km, 4):
                gn = min(4, km - g0)
                for j in range(gn):
                    op("t", lambda e, j=j, g0=g0: e.transpose(
                        pT[:, j * 128:(j + 1) * 128],
                        A16("lntok", (g0 + j) * 128, 128), idb[:]),
                        inc=tS if j == gn - 1 else None)
                wait("s", tS)
                if t > 0:
                    wait("s", tS, p1_zmm_t[t - 1])

                def evac_ln(e, g0=g0, gn=gn, ts=ts):
                    dst = a16[:, r16["ln0T"] + g0 * tile:
                              r16["ln0T"] + (g0 + gn) * tile].rearrange(
                        "p (k n) -> p k n", n=tile)[:, :, ts * 128:(ts + 1) * 128]
                    src = pT.rearrange("p (k n) -> p k n", n=128)[:, 0:gn, :]
                    return e.copy(dst, src)
                op("s", evac_ln, inc=sS)
                wait("t", sS)
            tr_done = tS.n

    for t in range(ntiles):
        if t == 0:
            for ts in range(tsn):
                LN_load(0, ts)
                LN_stats(0, ts)
        LN_finish(t)
        ln_done_s = sS.n

        wait("t", sS, ln_done_s)
        xi_evac = {}
        for m in range(dsl):
            px = p[m % 2]
            if m >= 2:
                wait("t", sS, xi_evac[m - 2])
            for k in range(km):
                op("t", lambda e, k=k, m=m, px=px: e.matmul(
                    px[:, 0:tile], win_ap(k)[:, m * 128:(m + 1) * 128],
                    A16("ln0T", k * tile, tile),
                    start=(k == 0), stop=(k == km - 1)),
                    inc=tS if k == km - 1 else None)
            wait("s", tS)
            if t > 0 and m < 2:
                wait("s", vS, p1_vend[t - 1])
            xi_evac[m] = op("s", lambda e, m=m, px=px: e.copy(
                A16("xiT", m * tile, tile), px[:, 0:tile]), inc=sS)

        wait("t", sS)
        proj_ev = {}
        tr_ev = {}
        for ts in range(tsn):
            ppj = p[2] if ts % 2 == 0 else p[0]
            if ts >= 2:
                wait("t", sS, proj_ev[ts % 2])
            for m in range(dsl):
                op("t", lambda e, m=m, ts=ts, ppj=ppj: e.matmul(
                    ppj[:, 0:pw],
                    a16[:, r16["xiT"] + m * tile + ts * 128:
                        r16["xiT"] + m * tile + ts * 128 + 128],
                    wxp[:, m * pw:(m + 1) * pw],
                    start=(m == 0), stop=(m == dsl - 1)),
                    inc=tS if m == dsl - 1 else None)
            wait("s", tS)
            proj_ev[ts % 2] = op("s", lambda e, ppj=ppj: e.copy(A32("proj", 0, pw), ppj[:, 0:pw]), inc=sS)
            op("s", lambda e: e.activation(A32("stat", 8, 1), A32("proj", 0, 1), AF.Abs), inc=sS)
            wait("s", sS)
            op("s", lambda e: e.activation(A32("stat", 8, 1), A32("stat", 8, 1), AF.Exp, scale=-1.0), inc=sS)
            wait("s", sS)
            op("s", lambda e: e.activation(A32("stat", 8, 1), A32("stat", 8, 1), AF.Ln, bias=1.0), inc=sS)
            wait("v", sS)
            op("v", lambda e: e.scalar_tensor_tensor(
                A32("stat", 9, 1), A32("proj", 0, 1), 0.0, A32("stat", 8, 1),
                ALU.max, ALU.add), inc=vS)
            wait("v", vS)
            op("v", lambda e: e.tensor_scalar(
                A32("stat", 10, 1), A32("stat", 9, 1), acol[:, 0:1], None, ALU.mult), inc=vS)
            wait("v", vS)
            ptr = p[6] if ts % 2 == 0 else p[3]
            if ts >= 2:
                wait("t", sS, tr_ev[ts % 2])
            wait("t", vS)
            op("t", lambda e, ptr=ptr: e.transpose(ptr[0:1, 0:128], A32("stat", 10, 1), idf[:]), inc=tS)
            wait("s", tS)
            op("s", lambda e, ts=ts, ptr=ptr: e.copy(larow[0:1, ts * 128:(ts + 1) * 128],
                                                     ptr[0:1, 0:128]), inc=sS)
            wait("t", sS)
            op("t", lambda e, ptr=ptr: e.transpose(ptr[0:ds, 0:128], A32("proj", 1, ds), idf[:]), inc=tS)
            op("t", lambda e, ptr=ptr: e.transpose(ptr[0:ds, 128:256], A32("proj", 1 + ds, ds), idf[:]), inc=tS)
            wait("s", tS)
            op("s", lambda e, ts=ts, ptr=ptr: e.copy(a16[0:ds, r16["bctB"] + ts * 128:r16["bctB"] + ts * 128 + 128],
                                                     ptr[0:ds, 0:128]), inc=sS)
            tr_ev[ts % 2] = op("s", lambda e, ts=ts, ptr=ptr: e.copy(
                a16[0:ds, r16["bctC"] + ts * 128:r16["bctC"] + ts * 128 + 128],
                ptr[0:ds, 128:256]), inc=sS)
        wait("t", sS)
        proj_done_s = sS.n

        wait("v", sS, proj_done_s)
        op("v", lambda e: e.tensor_tensor_scan(
            lsrow[:], onest[:], larow[:], lsst[:, 0:1], ALU.mult, ALU.add), inc=vS)
        wait("v", vS)
        op("v", lambda e: e.tensor_copy(lsst[:], lsrow[0:1, tile - 1:tile]), inc=vS)
        wait("v", vS)
        wait("t", vS)
        op("t", lambda e: e.matmul(p[5][:, 0:tile], onesr[:], larow[:], start=True, stop=True), inc=tS)
        wait("s", tS)
        if t > 0:
            wait("s", vS, p1_vend[t - 1])
        op("s", lambda e: e.activation(A32("abc", 0, tile), p[5][:, 0:tile], AF.Exp), inc=sS)
        wait("t", sS)
        op("t", lambda e: e.matmul(p[5][:, 0:tile], onesr[:], lsrow[:], start=True, stop=True), inc=tS)
        wait("s", tS)
        if t > 0:
            wait("s", gP)
        op("s", lambda e: e.activation(A16("qb", 0, tile), p[5][:, 0:tile], AF.Exp), inc=sS)
        aq_done_s = sS.n

        if t + 1 < ntiles and not cfg.get("noilv"):
            LN_load(t + 1, 0)
        wait("t", sS, aq_done_s)
        prev_silu = silu_done if t > 0 else {}
        prev_cev = cev if t > 0 else {}
        prev_vend = p1_vend.get(t - 1, 0)
        prev_pool = gP.n
        silu_done = {}
        bev = {}
        cev = {}
        ylk_done = {}
        pool_done = {}
        for m in range(dsl):
            pz = p[m % 2]
            pB = p[3] if m % 2 == 0 else p[5]
            pC = p[4] if m % 2 == 0 else p[6]
            if m == 0:
                wait("v", gs[0])
                wait("v", gs[1])
            if m >= 2:
                wait("t", sS, silu_done[m - 2])
                wait("t", sS, cev[m - 2])
                wait("s", vS, ylk_done[m - 2])
                wait("s", gP, pool_done[m - 2])
            elif t > 0:
                wait("t", sS, prev_silu[dsl - 2 + m])
                wait("t", sS, prev_cev[dsl - 2 + m])
                wait("s", vS, prev_vend)
                wait("s", gP, prev_pool)
            for k in range(km):
                op("t", lambda e, k=k, m=m, pz=pz: e.matmul(
                    pz[:, 0:tile], win_ap(k)[:, di + m * 128:di + (m + 1) * 128],
                    A16("ln0T", k * tile, tile),
                    start=(k == 0), stop=(k == km - 1)),
                    inc=tS if k == km - 1 else None)
            z_mm = tS.n
            op("t", lambda e, m=m, pB=pB: e.matmul(
                pB[:, 0:tile], wb[:, m * 128:(m + 1) * 128], a16[0:ds, r16["bctB"]:r16["bctB"] + tile],
                start=True, stop=True), inc=tS)
            b_mm = tS.n
            op("t", lambda e, m=m, pC=pC: e.matmul(
                pC[:, 0:tile], wc[:, m * 128:(m + 1) * 128], a16[0:ds, r16["bctC"]:r16["bctC"] + tile],
                start=True, stop=True), inc=tS)
            c_mm = tS.n
            szs = A16("sz", (m % 2) * tile, tile)
            wait("s", tS, z_mm)
            silu_done[m] = op("s", lambda e, szs=szs, pz=pz: e.activation(
                szs, pz[:, 0:tile], AF.Silu), inc=sS)
            wait("s", tS, b_mm)
            bev[m] = op("s", lambda e, pB=pB, m=m: e.copy(A16("bbf", (m % 2) * tile, tile), pB[:, 0:tile]), inc=sS)
            wait("s", tS, c_mm)
            cev[m] = op("s", lambda e, pC=pC, m=m: e.copy(A16("cbf", (m % 2) * tile, tile), pC[:, 0:tile]), inc=sS)
            bxs = A16("bx", 0, tile)
            wait("v", sS, bev[m])
            op("v", lambda e, bxs=bxs, m=m: e.tensor_tensor(
                bxs, A16("bbf", (m % 2) * tile, tile), A16("xiT", m * tile, tile), ALU.mult), inc=vS)
            wait("v", vS)
            hts = A32("hT", (m % 2) * tile, tile)
            op("v", lambda e, hts=hts, bxs=bxs, m=m: e.tensor_tensor_scan(
                hts, A32("abc", 0, tile), bxs, scst[:, m:m + 1], ALU.mult, ALU.add), inc=vS)
            wait("v", vS)
            op("v", lambda e, hts=hts, m=m: e.tensor_copy(
                scst[:, m:m + 1], hts[:, tile - 1:tile]), inc=vS)
            wait("v", vS)
            if m >= 4:
                wait("v", gs[(m // 2) % 2])
            wait("v", sS, cev[m])
            op("v", lambda e, hts=hts, m=m: e.tensor_tensor(
                A16("scrA", 0, tile), A16("cbf", (m % 2) * tile, tile), hts, ALU.mult), inc=vS)
            op("v", lambda e, m=m: e.scalar_tensor_tensor(
                A16("scrB", 0, tile), A16("xiT", m * tile, tile), dcol[:, m:m + 1],
                A16("scrA", 0, tile), ALU.mult, ALU.add), inc=vS)
            yls = A16("ylw", (m % 4) * tile, tile)
            op("v", lambda e, yls=yls, szs=szs: e.tensor_tensor(
                yls, A16("scrB", 0, tile), szs, ALU.mult), inc=vS)
            op("v", lambda e, szs=szs, m=m: e.tensor_tensor(
                A16("scrC", 0, tile), A16("cbf", (m % 2) * tile, tile), szs, ALU.mult), inc=vS)
            kws = A16("kw", (m % 4) * tile, tile)
            ylk_done[m] = op("v", lambda e, kws=kws: e.tensor_tensor(
                kws, A16("scrC", 0, tile), A16("qb", 0, tile), ALU.mult), inc=vS)
            pool_done[m] = gP.n
            if m % 2 == 1:
                par = (m // 2) % 2
                wait("g", vS)
                op("g", lambda e, t=t, m=m, par=par: e.dma_start(
                    out=sp_yl[t, :, (m - 1) * tile:(m + 1) * tile],
                    in_=A16("ylw", ((m - 1) % 4) * tile, 2 * tile)), inc=gs[par], k=16)
                op("g", lambda e, t=t, m=m, par=par: e.dma_start(
                    out=sp_k[t, :, (m - 1) * tile:(m + 1) * tile],
                    in_=A16("kw", ((m - 1) % 4) * tile, 2 * tile)), inc=gs[par], k=16)
            stride = max(1, dsl // tsn)
            if cfg.get("noilv"):
                pass
            elif t + 1 < ntiles and m % stride == stride - 1 and m // stride < tsn:
                j = m // stride
                if j + 1 < tsn:
                    LN_load(t + 1, j + 1)
                LN_stats(t + 1, j)
        if cfg.get("noilv") and t + 1 < ntiles:
            for ts_ in range(tsn):
                LN_load(t + 1, ts_)
                LN_stats(t + 1, ts_)
        p1_zmm_t[t] = z_mm
        p1_vend[t] = vS.n
    p1_t = tS.n

    op("v", lambda e: e.tensor_copy(ccsb[:, 0:dsl], scst[:]), inc=vS)
    wait("t", vS)
    op("t", lambda e: e.matmul(p[6][:, 0:1], onesr[:], lsst[:], start=True, stop=True), inc=tS)
    wait("s", tS)
    op("s", lambda e: e.copy(ccsb[:, dsl:dsl + 1], p[6][:, 0:1]), inc=sS)
    wait("g", sS)
    wait("g", vS)
    wait("g", gs[0])
    wait("g", gs[1])
    op("g", lambda e: e.dma_start(out=cc_in_t.ap(), in_=ccsb[:]), inc=gcc, k=16)
    wait("g", gcc)
    op("g", lambda e: e.collective_compute(
        "AllGather", ALU.bypass, replica_groups=[list(range(ncores))],
        ins=[cc_in_t.ap().opt()], outs=[cc_out_t.ap().opt()]), inc=cS)
    wait("g", cS)
    op("g", lambda e: e.dma_start(
        out=agsb[:], in_=cc_out_t.ap().rearrange("r p c -> p r c")), inc=gcc, k=16)

    wait("v", gcc)
    agv = agsb.rearrange("p (r c) -> p r c", c=dsl + 1)
    for j in range(ncores):
        op("v", lambda e, j=j: e.tensor_tensor(
            ctmp8[:], agv[:, :, dsl], cmaske[:, j * ncores:(j + 1) * ncores],
            ALU.mult), inc=vS)
        wait("v", vS)
        op("v", lambda e: e.tensor_reduce(cred[:], ctmp8[:], AX.X, ALU.add), inc=vS)
        wait("s", vS)
        op("s", lambda e: e.activation(cexp[:], cred[:], AF.Exp), inc=sS)
        wait("v", sS)
        op("v", lambda e, j=j: e.tensor_tensor(
            ccoef[:], cexp[:], cmaskm[:, j:j + 1], ALU.mult), inc=vS)
        wait("v", vS)
        op("v", lambda e, j=j: e.tensor_scalar(
            ctmp16[:], agsb[:, j * (dsl + 1):j * (dsl + 1) + dsl],
            ccoef[:, 0:1], None, ALU.mult), inc=vS)
        wait("v", vS)
        op("v", lambda e: e.tensor_tensor(hin[:], hin[:], ctmp16[:], ALU.add), inc=vS)
        wait("v", vS)
    barrier()

    if stop == "xchg":
        for s_ in (ycn, yx[0], yx[1], yst, gw, gs[0], gs[1], gcc):
            wait("y", s_)
        wait("y", cS, cS.n)
        return B.finalize()

    wait("g", tS, p1_t)
    for k in range(km):
        op("g", lambda e, k=k: e.dma_start(
            out=wi_ap(k), in_=wi_in[:, k * f:(k + 1) * f]), inc=gw, k=16)
    wi_gw = gw.n

    wait("y", gs[0])
    wait("y", gs[1])
    xb_rd = {}
    p2a_tile_v = 0
    p2a_tile_t = 0
    for t in range(ntiles):
        if t:
            wait("y", vS, p2a_tile_v)
            wait("y", tS, p2a_tile_t)
        op("y", lambda e, t=t: e.dma_start(
            out=a16[:, yl_in_off:yl_in_off + dsl * tile], in_=sp_yl[t]), inc=yyk, k=16)
        op("y", lambda e, t=t: e.dma_start(
            out=a16[:, k_in_off:k_in_off + dsl * tile], in_=sp_k[t]), inc=yyk, k=16)
        ymm_par = {}
        lnb_par = {}
        sq_par = {}
        stt_done = {}

        def A_stt(ts, t=t):
            par = ts % 2
            yb0 = y2_off + par * dsl * 128
            row = t * tile + ts * 128
            xb = A32("xb", par * dm, dm)
            if t or ts:
                wait("y", vS, xb_rd.get(par, 0))
            op("y", lambda e, row=row, xb=xb: e.dma_start(out=xb, in_=x_in[row:row + 128, :]), inc=yx[par], k=16)
            if ts < 2:
                wait("v", yyk)
            if ts >= 2:
                wait("v", tS, ymm_par[par])
            stt_done[ts] = []
            for m in range(dsl):
                v_ = op("v", lambda e, m=m, ts=ts, yb0=yb0: e.scalar_tensor_tensor(
                    a16[:, yb0 + m * 128:yb0 + (m + 1) * 128],
                    a16[:, k_in_off + m * tile + ts * 128:k_in_off + m * tile + ts * 128 + 128],
                    hin[:, m:m + 1],
                    a16[:, yl_in_off + m * tile + ts * 128:yl_in_off + m * tile + ts * 128 + 128],
                    ALU.mult, ALU.add), inc=vS)
                stt_done[ts].append(v_)

        def B_mm(ts):
            par = ts % 2
            pp = [p[2 * par + n] for n in range(n512)]
            yb0 = y2_off + par * dsl * 128
            if ts >= 2:
                wait("t", vS, lnb_par[par])
                wait("t", vS, sq_par[par][0])
                if sq_par[par][1]:
                    wait("t", sS, sq_par[par][1])
            for m in range(dsl):
                wait("t", vS, stt_done[ts][m])
                for n in range(n512):
                    op("t", lambda e, m=m, n=n, yb0=yb0, pp=pp: e.matmul(
                        pp[n][:, 0:512],
                        a16[:, yb0 + m * 128:yb0 + (m + 1) * 128],
                        wout_ap(m)[:, n * 512:(n + 1) * 512],
                        start=(m == 0), stop=(m == dsl - 1)),
                        inc=tS if (m == dsl - 1 and n == n512 - 1) else None)
            ymm_par[par] = tS.n

        def C_ln(ts, t=t):
            par = ts % 2
            pp = [p[2 * par + n] for n in range(n512)]
            wo_mm = ymm_par[par]

            def _w3(wo_mm=wo_mm):
                wait("v", tS, wo_mm)
            sq_par[par] = emit_ln_stats([(pp[n][:, 0:512], 512) for n in range(n512)],
                                        pre_wait=_w3)
            wait("v", sS)
            for n in range(n512):
                op("v", lambda e, n=n, pp=pp: e.tensor_scalar(
                    A32("lnb", n * 512, 512), pp[n][:, 0:512], A32("stat", 4, 1), A32("stat", 9, 1),
                    ALU.subtract, ALU.mult), inc=vS)
            lnb_par[par] = vS.n
            xb = A32("xb", par * dm, dm)
            wait("v", yx[par])
            wait("v", yst)
            op("v", lambda e, xb=xb: e.tensor_tensor(
                A32("fout", 0, dm), xb, A32("lnb", 0, dm), ALU.add), inc=vS)
            xb_rd[par] = vS.n
            wait("y", vS)
            op("y", lambda e, t=t, ts=ts: e.dma_start(
                out=sp_x1.ap()[t * tsn + ts], in_=A32("fout", 0, dm)), inc=yst, k=16)

        A_stt(0)
        if tsn > 1:
            A_stt(1)
        for ts in range(tsn):
            B_mm(ts)
            C_ln(ts)
            if ts + 2 < tsn:
                A_stt(ts + 2)
        p2a_tile_v = vS.n
        p2a_tile_t = tS.n
    p2a_yst = yst.n
    p2a_t = tS.n
    if stop == "p2a":
        for s_ in (ycn, yx[0], yx[1], yyk, yst, gw, gs[0], gs[1], gcc):
            wait("y", s_)
        wait("y", cS, cS.n)
        return B.finalize()

    wait("g", tS, p2a_t)
    for k in range(wo_in_b):
        op("g", lambda e, k=k: e.dma_start(
            out=wo_ap(k), in_=wo_in[:, k * dm:(k + 1) * dm]), inc=gw, k=16)

    wait("t", gw, wi_gw)
    wait("y", yst, p2a_yst)
    wi_x1_rd = 0
    wi_lnf_t = {}
    for t in range(ntiles):
        lnf0 = r16["ln0T"] if t % 2 == 0 else r16["ylw"]
        um0 = (t % 2) * tile
        if t:
            wait("y", vS, wi_x1_rd)
        op("y", lambda e, t=t: e.dma_start(
            out=A32("x1", 0, tsn * dm).rearrange("p (r c) -> p r c", c=dm),
            in_=sp_x1.ap()[t * tsn:(t + 1) * tsn].rearrange("r p c -> p r c")), inc=yxl, k=16)
        if t >= 2:
            wait("t", tS, wi_lnf_t[t % 2])
        for ts in range(tsn):
            x1b = A32("x1", ts * dm, dm)
            def _w2():
                wait("v", yxl)
            emit_ln_stats([(x1b, dm)], pre_wait=_w2)
            wait("v", sS)
            wait("v", yxl)
            op("v", lambda e, x1b=x1b: e.tensor_scalar(
                A16("lntok", 0, dm), x1b, A32("stat", 4, 1), A32("stat", 9, 1),
                ALU.subtract, ALU.mult), inc=vS)
            wait("t", vS)
            for g0 in range(0, km, 4):
                gn = min(4, km - g0)
                for j in range(gn):
                    op("t", lambda e, j=j, g0=g0: e.transpose(
                        pT[:, j * 128:(j + 1) * 128],
                        A16("lntok", (g0 + j) * 128, 128), idb[:]),
                        inc=tS if j == gn - 1 else None)
                wait("s", tS)

                def evac_lnf(e, g0=g0, gn=gn, ts=ts, lnf0=lnf0):
                    dst = a16[:, lnf0 + g0 * tile:
                              lnf0 + (g0 + gn) * tile].rearrange(
                        "p (k n) -> p k n", n=tile)[:, :, ts * 128:(ts + 1) * 128]
                    src = pT.rearrange("p (k n) -> p k n", n=128)[:, 0:gn, :]
                    return e.tensor_copy(dst, src)
                wait("v", tS)
                op("v", evac_lnf, inc=vS)
                wait("t", vS)
            wait("t", vS)
            for k in range(km):
                op("t", lambda e, k=k, ts=ts, lnf0=lnf0: e.matmul(
                    p[2][:, 0:e_], a16[:, lnf0 + k * tile + ts * 128:lnf0 + k * tile + ts * 128 + 128],
                    gatet[:, k * e_:(k + 1) * e_],
                    start=(k == 0), stop=(k == km - 1)),
                    inc=tS if k == km - 1 else None)
            wait("v", tS)
            op("v", lambda e: e.tensor_tensor(lg8[:], p[2][:, 0:e_], gateb[:], ALU.add), inc=vS)
            wait("v", vS)
            op("v", lambda e: e.tensor_reduce(mx1[:], lg8[:], AX.X, ALU.max), inc=vS)
            wait("v", vS)
            op("v", lambda e: e.tensor_scalar(pr8[:], lg8[:], mx1[:, 0:1], None, ALU.subtract), inc=vS)
            wait("v", vS)
            wait("s", vS)
            op("s", lambda e: e.activation(eq8[:], pr8[:], AF.Exp, scale=BIG), inc=sS)
            wait("v", sS)
            op("v", lambda e: e.tensor_tensor(pr8[:], eq8[:], iotar[:], ALU.mult), inc=vS)
            wait("v", vS)
            op("v", lambda e: e.tensor_reduce(mx1[:], pr8[:], AX.X, ALU.max), inc=vS)
            wait("v", vS)
            op("v", lambda e: e.tensor_scalar(pr8b[:], pr8[:], mx1[:, 0:1], None, ALU.subtract), inc=vS)
            wait("s", vS)
            op("s", lambda e: e.activation(oh8[:], pr8b[:], AF.Exp, scale=BIG), inc=sS)
            wait("v", sS)
            oh_v = vS.n
            if dbg:
                wait("y", vS)
                op("y", lambda e, t=t, ts=ts: e.dma_start(
                    out=sp_dbg.ap()[t, ts, :, 0:e_], in_=lg8[:]), inc=yst, k=16)
                op("y", lambda e, t=t, ts=ts: e.dma_start(
                    out=sp_dbg.ap()[t, ts, :, e_:2 * e_], in_=eq8[:]), inc=yst, k=16)
                op("y", lambda e, t=t, ts=ts: e.dma_start(
                    out=sp_dbg.ap()[t, ts, :, 2 * e_:3 * e_], in_=oh8[:]), inc=yst, k=16)
                op("y", lambda e, t=t, ts=ts: e.dma_start(
                    out=sp_dbg.ap()[t, ts, :, 3 * e_:3 * e_ + 1], in_=mx1[:, 0:1],
                    allow_slow_non_contiguous=True), inc=yst, k=16)
            wait("t", vS)
            for k in range(km):
                op("t", lambda e, k=k, ts=ts, lnf0=lnf0: e.matmul(
                    p[3][:, 0:er], a16[:, lnf0 + k * tile + ts * 128:lnf0 + k * tile + ts * 128 + 128],
                    lorat[:, k * er:(k + 1) * er],
                    start=(k == 0), stop=(k == km - 1)),
                    inc=tS if k == km - 1 else None)
            wait("v", tS)
            wait("v", vS, oh_v)
            for i in range(e_):
                op("v", lambda e, i=i: e.tensor_scalar(
                    umb[:, i * r_:(i + 1) * r_], p[3][:, i * r_:(i + 1) * r_],
                    oh8[:, i:i + 1], None, ALU.mult), inc=vS)
            wait("t", vS)
            op("t", lambda e: e.transpose(pT[0:er, 0:128], umb[:], idb[:]), inc=tS)
            wait("v", tS)
            op("v", lambda e, ts=ts, um0=um0: e.tensor_copy(umt[:, um0 + ts * 128:um0 + (ts + 1) * 128], pT[0:er, 0:128]), inc=vS)
            wait("t", vS)
        lnf_done_v = vS.n
        wi_x1_rd = vS.n
        wait("t", vS, lnf_done_v)
        relu_done = {}
        for mfm in range(fm):
            pb = p[mfm % 2]
            if mfm >= 2:
                wait("t", sS, relu_done[mfm - 2])
            for k in range(km):
                op("t", lambda e, k=k, mfm=mfm, pb=pb, lnf0=lnf0: e.matmul(
                    pb[:, 0:tile], wi_ap(k)[:, mfm * 128:(mfm + 1) * 128],
                    a16[:, lnf0 + k * tile:lnf0 + (k + 1) * tile],
                    start=(k == 0), stop=False))
            op("t", lambda e, mfm=mfm, pb=pb, um0=um0: e.matmul(
                pb[:, 0:tile], lorab[:, mfm * 128:(mfm + 1) * 128], umt[:, um0:um0 + tile],
                start=False, stop=True), inc=tS)
            wait("s", tS)
            if mfm >= 2:
                wait("s", gi[mfm % 2])
            ists = A16("ist", (mfm % 2) * tile, tile)
            relu_done[mfm] = op("s", lambda e, ists=ists, mfm=mfm, pb=pb: e.activation(
                ists, pb[:, 0:tile], AF.Relu, bias=wibcol[:, mfm:mfm + 1]), inc=sS)
            wait("g", sS)
            op("g", lambda e, t=t, mfm=mfm, ists=ists: e.dma_start(
                out=sp_int.ap()[t, :, mfm, :], in_=ists), inc=gi[mfm % 2], k=16)
        wi_lnf_t[t % 2] = tS.n
    p2bwi_t = tS.n

    wait("g", tS, p2bwi_t)
    for k in range(wo_in_b, fm):
        op("g", lambda e, k=k: e.dma_start(
            out=wo_ap(k), in_=wo_in[:, k * dm:(k + 1) * dm]), inc=gw, k=16)

    wait("t", gw)
    wait("y", gi[0])
    wait("y", gi[1])
    wo_tile_t = 0
    wo_tile_v = 0
    for t in range(ntiles):
        if t:
            wait("y", tS, wo_tile_t)
            wait("y", vS, wo_tile_v)
        op("y", lambda e, t=t: e.dma_start(
            out=a16[:, 0:fm * tile],
            in_=sp_int.ap()[t].rearrange("p k n -> p (k n)")), inc=yil, k=16)
        op("y", lambda e, t=t: e.dma_start(
            out=A32("x1", 0, tsn * dm).rearrange("p (r c) -> p r c", c=dm),
            in_=sp_x1.ap()[t * tsn:(t + 1) * tsn].rearrange("r p c -> p r c")), inc=yxl, k=16)
        fo_par = {}
        for ts in range(tsn):
            par = ts % 2
            pp = [p[2 * par + n] for n in range(n512)]
            wait("t", yil)
            wait("v", yxl)
            if ts >= 2:
                wait("t", vS, fo_par[par])
            for n in range(n512):
                for k in range(fm):
                    op("t", lambda e, n=n, k=k, ts=ts, pp=pp: e.matmul(
                        pp[n][:, 0:512],
                        a16[:, k * tile + ts * 128:k * tile + ts * 128 + 128],
                        wo_ap(k)[:, n * 512:(n + 1) * 512],
                        start=(k == 0), stop=(k == fm - 1)),
                        inc=tS if k == fm - 1 else None)
            wo_mm = tS.n
            wait("v", tS, wo_mm)
            for n in range(n512):
                op("v", lambda e, n=n, pp=pp: e.tensor_tensor(A32("fout", n * 512, 512), pp[n][:, 0:512],
                                                              wobb[:, n * 512:(n + 1) * 512], ALU.add), inc=vS)
            fo_par[par] = vS.n
            emit_ln_stats([(A32("fout", 0, dm), dm)])
            wait("v", sS)
            ob = A32("outb", 0, dm)
            op("v", lambda e: e.tensor_scalar(
                A32("lnb", 0, dm), A32("fout", 0, dm), A32("stat", 4, 1), A32("stat", 9, 1),
                ALU.subtract, ALU.mult), inc=vS)
            wait("v", yst)
            op("v", lambda e, ob=ob, ts=ts: e.tensor_tensor(
                ob, A32("x1", ts * dm, dm), A32("lnb", 0, dm), ALU.add), inc=vS)
            wait("y", vS)
            row = t * tile + ts * 128
            op("y", lambda e, row=row, ob=ob: e.dma_start(
                out=o_out[row:row + 128, :], in_=ob), inc=yst, k=16)
            if ts == tsn - 1:
                wo_tile_t = tS.n
                wo_tile_v = vS.n

    for s_ in (ycn, yx[0], yx[1], yyk, yxl, yil, yst, gw, gs[0], gs[1], gi[0], gi[1], gcc):
        wait("y", s_)
    wait("y", cS, cS.n)
    return B.finalize()


def make_core_inputs(cfg, inp, core, shared_cache):
    dm, di, ds, f, e_, r_ = (cfg[k] for k in ["dm", "di", "ds", "f", "e", "r"])
    nt, ncores, nbatch = cfg["nt"], cfg["ncores"], cfg["nbatch"]
    km, dsl, fm, er = dm // 128, di // 128, f // 128, e_ * r_
    chain = ncores // nbatch

    C = np.ascontiguousarray

    def ktile(w, kdim):
        n = w.shape[1]
        return C(w.reshape(kdim, 128, n).transpose(1, 0, 2).reshape(128, kdim * n))

    if not shared_cache:
        shared_cache["x"] = np.asarray(inp["L0_attn_x"], np.float32).reshape(ncores, nt, dm)
        W_in = np.asarray(inp["L0_attn_W_in"], np.float32)
        W_xp = np.asarray(inp["L0_attn_W_xp"], np.float32)
        W_out = np.asarray(inp["L0_attn_W_out"], np.float32)
        wi_W = np.asarray(inp["L0_ffn_wi_W"], np.float32)
        wo_W = np.asarray(inp["L0_ffn_wo_W"], np.float32)
        gate_W = np.asarray(inp["L0_ffn_gate_W"], np.float32)
        lora_A = np.asarray(inp["L0_ffn_lora_A"], np.float32)
        lora_B = np.asarray(inp["L0_ffn_lora_B"], np.float32)
        D = np.asarray(inp["L0_attn_D"], np.float32)
        wi_b = np.asarray(inp["L0_ffn_wi_b"], np.float32)
        wo_b = np.asarray(inp["L0_ffn_wo_b"], np.float32)
        gate_b = np.asarray(inp["L0_ffn_gate_b"], np.float32)
        shared_cache["w"] = {
            "w_win": ktile(W_in, km),
            "w_wxp": ktile(W_xp, dsl),
            "w_wb": C(np.asarray(inp["L0_attn_W_B"], np.float32)),
            "w_wc": C(np.asarray(inp["L0_attn_W_C"], np.float32)),
            "w_wout": ktile(W_out, dsl),
            "w_wi": ktile(C(wi_W.T), km),
            "w_wo": ktile(C(wo_W.T), fm),
            "w_gatet": ktile(C(gate_W.T), km),
            "w_lorat": ktile(C(lora_A.reshape(er, dm).T), km),
            "w_lorab": C(lora_B.transpose(0, 2, 1).reshape(er, f)),
            "w_dcol": C(D.reshape(dsl, 128).T),
            "w_wibcol": C(wi_b.reshape(fm, 128).T),
            "w_wobb": C(np.broadcast_to(wo_b, (128, dm))),
            "w_gateb": C(np.broadcast_to(gate_b, (128, e_))),
            "w_iotar": C(np.broadcast_to((e_ - np.arange(e_, dtype=np.float32)).reshape(1, e_), (128, e_))),
            "w_alog": np.asarray(inp["L0_attn_A_log"], np.float32).reshape(1, 1),
            "w_ident": np.eye(128, dtype=np.float32),
            "w_onesr": np.ones((1, 128), np.float32),
        }
    b0 = (core // chain) * chain
    me = np.zeros((ncores, ncores), np.float32)
    mmv = np.zeros((ncores,), np.float32)
    for j in range(b0, core):
        mmv[j] = 1.0
        for k in range(j + 1, core):
            me[j, k] = 1.0
    d = dict(shared_cache["w"])
    d["x"] = C(shared_cache["x"][core])
    d["w_cmaske"] = C(np.broadcast_to(me.reshape(1, -1), (128, ncores * ncores)))
    d["w_cmaskm"] = C(np.broadcast_to(mmv.reshape(1, -1), (128, ncores)))
    return d


_PROGRAM_CACHE = {}


def kernel(**inputs):
    cfg = default_cfg()
    if "main" not in _PROGRAM_CACHE:
        _PROGRAM_CACHE["main"] = build_program(cfg)
    nc = _PROGRAM_CACHE["main"]
    ncores = cfg["ncores"]
    shared = {}
    in_maps = [make_core_inputs(cfg, inputs, c, shared) for c in range(ncores)]
    res = run_bass_kernel_spmd(nc, in_maps, core_ids=list(range(ncores)))
    nt, dm = cfg["nt"], cfg["dm"]
    full = np.empty((ncores, nt, dm), np.float32)
    for c in range(ncores):
        full[c] = res.results[c]["o"].reshape(nt, dm)
    return full.reshape(cfg["nbatch"], (ncores // cfg["nbatch"]) * nt, dm)


# version 23 (final)
# speedup vs baseline: 1.1421x; 1.0120x over previous
import numpy as np

import concourse.bass as bass
import concourse.mybir as mybir
from concourse.bass_utils import run_bass_kernel_spmd

F32 = mybir.dt.float32
BF16 = mybir.dt.bfloat16
AF = mybir.ActivationFunctionType
ALU = mybir.AluOpType
AX = mybir.AxisListType

BIG = 1.0e30


def default_cfg():
    return dict(dm=1024, di=2048, ds=64, f=4096, e=8, r=4,
                nt=4096, tile=512, ncores=8, nbatch=2)


class Sem:
    def __init__(self, h):
        self.h = h
        self.n = 0


class Builder:

    def __init__(self, racecheck=False):
        self.nc = bass.Bass("TRN2", target_bir_lowering=False,
                            detect_race_conditions=racecheck)
        self.q = {"t": [], "v": [], "s": [], "y": [], "g": []}
        self._ctx = []

    def ent(self, t):
        self._ctx.append(t)
        return t.__enter__()

    def sb(self, nm, shape, dt):
        return self.ent(self.nc.sbuf_tensor(nm, shape, dt))

    def ps(self, nm, shape, dt=F32):
        return self.ent(self.nc.psum_tensor(nm, shape, dt))

    def sem(self, nm):
        return Sem(self.ent(self.nc.semaphore(nm)))

    def op(self, eng, fn, inc=None, k=1):
        if inc is None:
            self.q[eng].append(fn)
            return None
        inc.n += k
        self.q[eng].append(lambda e, f=fn, s=inc.h, kk=k: f(e).then_inc(s, kk))
        return inc.n

    def wait(self, eng, sem, val=None):
        v = sem.n if val is None else val
        if v <= 0:
            return
        self.q[eng].append(lambda e, s=sem.h, vv=v: e.wait_ge(s, vv))

    def finalize(self):
        nc = self.nc
        q = self.q
        with nc.Block() as block:
            @block.tensor
            def _(e):
                for f in q["t"]:
                    f(e)

            @block.vector
            def _(e):
                for f in q["v"]:
                    f(e)

            @block.scalar
            def _(e):
                for f in q["s"]:
                    f(e)

            @block.sync
            def _(e):
                for f in q["y"]:
                    f(e)

            @block.gpsimd
            def _(e):
                for f in q["g"]:
                    f(e)
        for t in reversed(self._ctx):
            t.__exit__(None, None, None)
        return nc


def build_program(cfg):
    dm, di, ds, f, e_, r_ = (cfg[k] for k in ["dm", "di", "ds", "f", "e", "r"])
    nt, tile, ncores, nbatch = (cfg[k] for k in ["nt", "tile", "ncores", "nbatch"])
    stop = cfg.get("stop", "all")
    km = dm // 128
    dsl = di // 128
    fm = f // 128
    er = e_ * r_
    pw = 2 * ds + 1
    ntiles = nt // tile
    tsn = tile // 128
    nrow = nt // 128
    n512 = dm // 512

    B = Builder(racecheck=cfg.get("racecheck", False))
    nc = B.nc
    op, wait = B.op, B.wait

    def din(nm, shape, dt=F32):
        return nc.dram_tensor(nm, shape, dt, kind="ExternalInput").ap()

    x_in = din("x", [nt, dm])
    win_in = din("w_win", [128, km * 2 * di])
    wxp_in = din("w_wxp", [128, dsl * pw])
    wb_in = din("w_wb", [ds, di])
    wc_in = din("w_wc", [ds, di])
    wout_in = din("w_wout", [128, dsl * dm])
    wi_in = din("w_wi", [128, km * f])
    wo_in = din("w_wo", [128, fm * dm])
    gatet_in = din("w_gatet", [128, km * e_])
    lorat_in = din("w_lorat", [128, km * er])
    lorab_in = din("w_lorab", [er, f])
    dcol_in = din("w_dcol", [128, dsl])
    wibcol_in = din("w_wibcol", [128, fm])
    wobb_in = din("w_wobb", [128, dm])
    gateb_in = din("w_gateb", [128, e_])
    alog_in = din("w_alog", [1, 1])
    ident_in = din("w_ident", [128, 128])
    onesr_in = din("w_onesr", [1, 128])
    iotar_in = din("w_iotar", [128, e_])
    cmaske_in = din("w_cmaske", [128, ncores * ncores])
    cmaskm_in = din("w_cmaskm", [128, ncores])

    o_out = nc.dram_tensor("o", [nt, dm], F32, kind="ExternalOutput").ap()

    dbg = cfg.get("dbg", False)
    dbgkw = dict(kind="ExternalOutput") if dbg else {}
    sp_yl = nc.dram_tensor("sp_yl", [ntiles, 128, dsl * tile], BF16, **dbgkw).ap()
    sp_k = nc.dram_tensor("sp_k", [ntiles, 128, dsl * tile], BF16, **dbgkw).ap()
    sp_x1 = nc.dram_tensor("sp_x1", [nrow, 128, dm], F32, **dbgkw)
    sp_int = nc.dram_tensor("sp_int", [ntiles, 128, fm, tile], BF16, **dbgkw)
    sp_dbg = nc.dram_tensor("sp_dbg", [ntiles, tsn, 128, 4 * e_], F32,
                            **dbgkw) if dbg else None
    cc_in_t = nc.dram_tensor("cc_in", [128, dsl + 1], F32)
    cc_out_t = nc.dram_tensor("cc_out", [ncores, 128, dsl + 1], F32)

    acols = max(km * 2 * di, km * f)
    bcols = dsl * dm
    wo_in_b = min(fm, bcols // dm)
    wbig = B.sb("wbig", [128, acols + bcols], BF16)

    def win_ap(k):
        return wbig[:, k * 2 * di:(k + 1) * 2 * di]

    def wout_ap(k):
        return wbig[:, acols + k * dm:acols + (k + 1) * dm]

    def wi_ap(k):
        return wbig[:, k * f:(k + 1) * f]

    def wo_ap(k):
        if k < wo_in_b:
            return wbig[:, acols + k * dm:acols + (k + 1) * dm]
        kk = k - wo_in_b
        return wbig[:, kk * dm:(kk + 1) * dm]

    wxp = B.sb("wxp", [128, dsl * pw], BF16)
    wb = B.sb("wb", [ds, di], BF16)
    wc = B.sb("wc", [ds, di], BF16)
    lorab = B.sb("lorab", [er, f], BF16)
    lorat = B.sb("lorat", [128, km * er], BF16)
    gatet = B.sb("gatet", [128, km * e_], BF16)
    dcol = B.sb("dcol", [128, dsl], F32)
    wibcol = B.sb("wibcol", [128, fm], F32)
    wobb = B.sb("wobb", [128, dm], BF16)
    gateb = B.sb("gateb", [128, e_], F32)
    idb = B.sb("idb", [128, 128], BF16)
    idf = B.sb("idf", [128, 128], F32)
    onesr = B.sb("onesr", [1, 128], F32)
    onest = B.sb("onest", [1, tile], F32)
    alog_sb = B.sb("alog_sb", [1, 1], F32)
    acol = B.sb("acol", [128, 1], F32)
    iotar = B.sb("iotar", [128, e_], F32)
    cmaske = B.sb("cmaske", [128, ncores * ncores], F32)
    cmaskm = B.sb("cmaskm", [128, ncores], F32)
    ccsb = B.sb("ccsb", [128, dsl + 1], F32)
    agsb = B.sb("agsb", [128, ncores * (dsl + 1)], F32)
    hin = B.sb("hin", [128, dsl], F32)
    scst = B.sb("scst", [128, dsl], F32)
    lsst = B.sb("lsst", [1, 1], F32)
    larow = B.sb("larow", [1, tile], F32)
    lsrow = B.sb("lsrow", [1, tile], F32)
    umt = B.sb("umt", [er, 2 * tile], BF16)
    umb = B.sb("umb", [128, er], BF16)
    oh8 = B.sb("oh8", [128, e_], F32)
    eq8 = B.sb("eq8", [128, e_], F32)
    pr8 = B.sb("pr8", [128, e_], F32)
    pr8b = B.sb("pr8b", [128, e_], F32)
    lg8 = B.sb("lg8", [128, e_], F32)
    mx1 = B.sb("mx1", [128, 1], F32)
    cred = B.sb("cred", [128, 1], F32)
    cexp = B.sb("cexp", [128, 1], F32)
    ccoef = B.sb("ccoef", [128, 1], F32)
    ctmp8 = B.sb("ctmp8", [128, ncores], F32)
    ctmp16 = B.sb("ctmp16", [128, dsl], F32)

    r16 = {}
    off = 0
    for nm, sz in [("xiT", dsl * tile), ("ln0T", km * tile), ("lntok", dm),
                   ("bctB", tile), ("bctC", tile), ("bx", tile),
                   ("sz", 2 * tile), ("ylw", 4 * tile), ("kw", 4 * tile),
                   ("qb", tile), ("scrA", tile), ("scrB", tile), ("scrC", tile),
                   ("bbf", 2 * tile), ("cbf", 2 * tile)]:
        r16[nm] = off
        off += sz
    r16["ist"] = r16["scrA"]
    a16cols = max(off, fm * tile, 2 * dsl * tile + 2 * dsl * 128)
    a16 = B.sb("a16", [128, a16cols], BF16)

    def A16(nm, lo, sz):
        return a16[:, r16[nm] + lo:r16[nm] + lo + sz]

    yl_in_off = 0
    k_in_off = dsl * tile
    y2_off = 2 * dsl * tile

    r32 = {}
    off = 0
    for nm, sz in [("stat", 26), ("proj", pw + 3), ("fout", dm), ("lnb", dm),
                   ("U", max(tile + 2 * tile + 2 * dm, tsn * dm + dm))]:
        r32[nm] = off
        off += sz
    r32["sq"] = r32["lnb"]
    r32["abc"] = r32["U"]
    r32["hT"] = r32["U"] + tile
    r32["xb"] = r32["U"] + 3 * tile
    r32["x1"] = r32["U"]
    r32["outb"] = r32["U"] + tsn * dm
    a32 = B.sb("a32", [128, off], F32)

    def A32(nm, lo, sz):
        return a32[:, r32[nm] + lo:r32[nm] + lo + sz]

    p = [B.ps("p%d" % i, [128, 512]) for i in range(7)]
    pT = B.ps("pT", [128, 512], BF16)
    epsc = B.sb("epsc", [128, 1], F32)

    ycn = B.sem("ycn")
    yx = [B.sem("yx0"), B.sem("yx1")]
    yyk = B.sem("yyk")
    yxl = B.sem("yxl")
    yic = [B.sem("yic%d" % i) for i in range(4)]
    yst = B.sem("yst")
    gw = B.sem("gw")
    gs = [B.sem("gs0"), B.sem("gs1")]
    gi = [B.sem("gi0"), B.sem("gi1")]
    gcc = B.sem("gcc")
    gP = B.sem("gP")
    tS = B.sem("tS")
    vS = B.sem("vS")
    sS = B.sem("sS")
    cS = B.sem("cS")


    def emit_ln_stats(srcs, pre_wait=None, m_slot=4, r_slot=9, defer_rstd=False):
        if pre_wait is not None:
            pre_wait()
        is_psum = bool(srcs and getattr(srcs[0][0].tensor, "name", "").startswith("p"))
        for i, (ap_, w_) in enumerate(srcs):
            op("v", lambda e, ap_=ap_, i=i: e.tensor_reduce(
                A32("stat", 0 + i, 1), ap_, AX.X, ALU.add), inc=vS)
        if is_psum:
            wait("s", vS)
            for i, (ap_, w_) in enumerate(srcs):
                op("s", lambda e, ap_=ap_, w_=w_, i=i: e.activation(
                    A32("sq", 0, w_), ap_, AF.Square,
                    accum_out=A32("stat", 2 + i, 1)), inc=sS)
            reads_done = (vS.n, sS.n)
            wait("v", sS)
        else:
            for i, (ap_, w_) in enumerate(srcs):
                op("v", lambda e, ap_=ap_, w_=w_, i=i: e.tensor_tensor(
                    A32("sq", 0, w_), ap_, ap_, ALU.mult), inc=vS)
                op("v", lambda e, w_=w_, i=i: e.tensor_reduce(
                    A32("stat", 2 + i, 1), A32("sq", 0, w_), AX.X, ALU.add), inc=vS)
            reads_done = (vS.n, 0)
        wait("v", vS)
        if len(srcs) == 2:
            op("v", lambda e: e.tensor_tensor(A32("stat", 0, 1), A32("stat", 0, 1), A32("stat", 1, 1), ALU.add), inc=vS)
            op("v", lambda e: e.tensor_tensor(A32("stat", 2, 1), A32("stat", 2, 1), A32("stat", 3, 1), ALU.add), inc=vS)
            wait("v", vS)
        op("v", lambda e, m_slot=m_slot: e.tensor_scalar(A32("stat", m_slot, 1), A32("stat", 0, 1), 1.0 / dm, None, ALU.mult), inc=vS)
        op("v", lambda e: e.tensor_scalar(A32("stat", 5, 1), A32("stat", 2, 1), 1.0 / dm, None, ALU.mult), inc=vS)
        wait("v", vS)
        op("v", lambda e, m_slot=m_slot: e.tensor_tensor(A32("stat", 6, 1), A32("stat", m_slot, 1), A32("stat", m_slot, 1), ALU.mult), inc=vS)
        wait("v", vS)
        if defer_rstd:
            op("v", lambda e, r_slot=r_slot: e.tensor_scalar(
                A32("stat", r_slot, 1), A32("stat", 5, 1), A32("stat", 6, 1), 1e-5,
                ALU.subtract, ALU.add), inc=vS)
            return reads_done
        op("v", lambda e: e.tensor_scalar(A32("stat", 7, 1), A32("stat", 5, 1), A32("stat", 6, 1), 1e-5,
                                          ALU.subtract, ALU.add), inc=vS)
        wait("s", vS)
        op("s", lambda e: e.activation(A32("stat", 7, 1), A32("stat", 7, 1), AF.Ln), inc=sS)
        wait("s", sS)
        op("s", lambda e, r_slot=r_slot: e.activation(A32("stat", r_slot, 1), A32("stat", 7, 1), AF.Exp, scale=-0.5), inc=sS)
        return reads_done

    def barrier():
        tv, vv, sv = tS.n, vS.n, sS.n
        for eng in ("t", "v", "s", "y", "g"):
            wait(eng, tS, tv)
            wait(eng, vS, vv)
            wait(eng, sS, sv)

    for nm, dst, src in [("idf", idf, ident_in), ("onesr", onesr, onesr_in),
                         ("dcol", dcol, dcol_in), ("wibcol", wibcol, wibcol_in),
                         ("gateb", gateb, gateb_in), ("iotar", iotar, iotar_in),
                         ("alog", alog_sb, alog_in), ("cmaske", cmaske, cmaske_in),
                         ("cmaskm", cmaskm, cmaskm_in)]:
        op("y", lambda e, d=dst, s=src: e.dma_start(out=d[:], in_=s), inc=ycn, k=16)
    op("g", lambda e: e.dma_start(out=wobb[:], in_=wobb_in), inc=gw, k=16)
    op("g", lambda e: e.dma_start(out=wbig[:, 0:km * 2 * di], in_=win_in), inc=gw, k=16)
    op("g", lambda e: e.dma_start(out=wxp[:], in_=wxp_in), inc=gw, k=16)
    op("g", lambda e: e.dma_start(out=wb[:], in_=wb_in), inc=gw, k=16)
    op("g", lambda e: e.dma_start(out=wc[:], in_=wc_in), inc=gw, k=16)
    op("g", lambda e: e.dma_start(out=idb[:], in_=ident_in), inc=gw, k=16)
    op("g", lambda e: e.dma_start(out=gatet[:], in_=gatet_in), inc=gw, k=16)
    op("g", lambda e: e.dma_start(out=lorat[:], in_=lorat_in), inc=gw, k=16)
    op("g", lambda e: e.dma_start(out=lorab[:], in_=lorab_in), inc=gw, k=16)
    for k in range(dsl):
        op("g", lambda e, k=k: e.dma_start(
            out=wout_ap(k), in_=wout_in[:, k * dm:(k + 1) * dm]), inc=gw, k=16)

    wait("s", ycn)
    op("s", lambda e: e.activation(alog_sb[:], alog_sb[:], AF.Exp), inc=sS)
    wait("v", sS)
    op("v", lambda e: e.tensor_scalar(alog_sb[:], alog_sb[:], -1.0, None, ALU.mult), inc=vS)
    op("v", lambda e: e.memset(onest[:], 1.0), inc=vS)
    op("v", lambda e: e.memset(epsc[:], 1e-5), inc=vS)
    op("v", lambda e: e.memset(scst[:], 0.0), inc=vS)
    op("v", lambda e: e.memset(lsst[:], 0.0), inc=vS)
    op("v", lambda e: e.memset(hin[:], 0.0), inc=vS)
    wait("t", vS)
    op("t", lambda e: e.matmul(p[6][:, 0:1], onesr[:], alog_sb[:], start=True, stop=True), inc=tS)
    wait("s", tS)
    op("s", lambda e: e.copy(acol[:], p[6][:, 0:1]), inc=sS)

    wait("t", gw)
    wait("v", gw)
    wait("s", gw)
    wait("t", ycn)
    wait("v", ycn)
    wait("s", ycn)
    barrier()

    xb_guard = {}
    ln_stat_v = {}
    p1_zmm_t = {}
    p1_vend = {}

    def LN_load(t, ts):
        row = t * tile + ts * 128
        xb = A32("xb", (ts % 2) * dm, dm)
        g = xb_guard.get(ts % 2)
        if g:
            wait("y", vS, g)
        op("y", lambda e, row=row, xb=xb: e.dma_start(
            out=xb, in_=x_in[row:row + 128, :]), inc=yx[ts % 2], k=16)

    def LN_stats(t, ts):
        xb = A32("xb", (ts % 2) * dm, dm)

        def _w(ts=ts):
            wait("v", yx[ts % 2])
        emit_ln_stats([(xb, dm)], pre_wait=_w, m_slot=16 + 2 * ts, r_slot=17 + 2 * ts,
                      defer_rstd=True)
        xb_guard[ts % 2] = vS.n
        ln_stat_v[(t, ts)] = (vS.n, sS.n)

    def LN_finish(t):
        for ts in range(tsn):
            wait("s", vS, ln_stat_v[(t, ts)][0])
            op("s", lambda e, ts=ts: e.activation(A32("stat", 17 + 2 * ts, 1),
                                                  A32("stat", 17 + 2 * ts, 1), AF.Ln), inc=sS)
            wait("s", sS)
            op("s", lambda e, ts=ts: e.activation(A32("stat", 17 + 2 * ts, 1),
                                                  A32("stat", 17 + 2 * ts, 1), AF.Exp, scale=-0.5), inc=sS)
        rstd_s = sS.n
        tr_done = 0
        for ts in range(tsn):
            xb = A32("xb", (ts % 2) * dm, dm)
            row = t * tile + ts * 128
            g = xb_guard.get(ts % 2)
            if g:
                wait("y", vS, g)
            op("y", lambda e, row=row, xb=xb: e.dma_start(
                out=xb, in_=x_in[row:row + 128, :]), inc=yx[ts % 2], k=16)
            wait("v", yx[ts % 2])
            wait("v", sS, rstd_s)
            if tr_done:
                wait("v", tS, tr_done)
            op("v", lambda e, xb=xb, ts=ts: e.tensor_scalar(
                A16("lntok", 0, dm), xb, A32("stat", 16 + 2 * ts, 1), A32("stat", 17 + 2 * ts, 1),
                ALU.subtract, ALU.mult), inc=vS)
            xb_guard[ts % 2] = vS.n
            wait("t", vS)
            if t > 0 and ts == 0:
                pass
            for g0 in range(0, km, 4):
                gn = min(4, km - g0)
                for j in range(gn):
                    op("t", lambda e, j=j, g0=g0: e.transpose(
                        pT[:, j * 128:(j + 1) * 128],
                        A16("lntok", (g0 + j) * 128, 128), idb[:]),
                        inc=tS if j == gn - 1 else None)
                wait("s", tS)
                if t > 0:
                    wait("s", tS, p1_zmm_t[t - 1])

                def evac_ln(e, g0=g0, gn=gn, ts=ts):
                    dst = a16[:, r16["ln0T"] + g0 * tile:
                              r16["ln0T"] + (g0 + gn) * tile].rearrange(
                        "p (k n) -> p k n", n=tile)[:, :, ts * 128:(ts + 1) * 128]
                    src = pT.rearrange("p (k n) -> p k n", n=128)[:, 0:gn, :]
                    return e.copy(dst, src)
                op("s", evac_ln, inc=sS)
                wait("t", sS)
            tr_done = tS.n

    for t in range(ntiles):
        if t == 0:
            for ts in range(tsn):
                LN_load(0, ts)
                LN_stats(0, ts)
        LN_finish(t)
        ln_done_s = sS.n

        wait("t", sS, ln_done_s)
        xi_evac = {}
        for m in range(dsl):
            px = p[m % 2]
            if m >= 2:
                wait("t", sS, xi_evac[m - 2])
            for k in range(km):
                op("t", lambda e, k=k, m=m, px=px: e.matmul(
                    px[:, 0:tile], win_ap(k)[:, m * 128:(m + 1) * 128],
                    A16("ln0T", k * tile, tile),
                    start=(k == 0), stop=(k == km - 1)),
                    inc=tS if k == km - 1 else None)
            wait("s", tS)
            if t > 0 and m < 2:
                wait("s", vS, p1_vend[t - 1])
            xi_evac[m] = op("s", lambda e, m=m, px=px: e.copy(
                A16("xiT", m * tile, tile), px[:, 0:tile]), inc=sS)

        wait("t", sS)
        proj_ev = {}
        tr_ev = {}
        for ts in range(tsn):
            ppj = p[2] if ts % 2 == 0 else p[0]
            if ts >= 2:
                wait("t", sS, proj_ev[ts % 2])
            for m in range(dsl):
                op("t", lambda e, m=m, ts=ts, ppj=ppj: e.matmul(
                    ppj[:, 0:pw],
                    a16[:, r16["xiT"] + m * tile + ts * 128:
                        r16["xiT"] + m * tile + ts * 128 + 128],
                    wxp[:, m * pw:(m + 1) * pw],
                    start=(m == 0), stop=(m == dsl - 1)),
                    inc=tS if m == dsl - 1 else None)
            wait("s", tS)
            proj_ev[ts % 2] = op("s", lambda e, ppj=ppj: e.copy(A32("proj", 0, pw), ppj[:, 0:pw]), inc=sS)
            op("s", lambda e: e.activation(A32("stat", 8, 1), A32("proj", 0, 1), AF.Abs), inc=sS)
            wait("s", sS)
            op("s", lambda e: e.activation(A32("stat", 8, 1), A32("stat", 8, 1), AF.Exp, scale=-1.0), inc=sS)
            wait("s", sS)
            op("s", lambda e: e.activation(A32("stat", 8, 1), A32("stat", 8, 1), AF.Ln, bias=1.0), inc=sS)
            wait("v", sS)
            op("v", lambda e: e.scalar_tensor_tensor(
                A32("stat", 9, 1), A32("proj", 0, 1), 0.0, A32("stat", 8, 1),
                ALU.max, ALU.add), inc=vS)
            wait("v", vS)
            op("v", lambda e: e.tensor_scalar(
                A32("stat", 10, 1), A32("stat", 9, 1), acol[:, 0:1], None, ALU.mult), inc=vS)
            wait("v", vS)
            ptr = p[6] if ts % 2 == 0 else p[3]
            if ts >= 2:
                wait("t", sS, tr_ev[ts % 2])
            wait("t", vS)
            op("t", lambda e, ptr=ptr: e.transpose(ptr[0:1, 0:128], A32("stat", 10, 1), idf[:]), inc=tS)
            wait("s", tS)
            op("s", lambda e, ts=ts, ptr=ptr: e.copy(larow[0:1, ts * 128:(ts + 1) * 128],
                                                     ptr[0:1, 0:128]), inc=sS)
            wait("t", sS)
            op("t", lambda e, ptr=ptr: e.transpose(ptr[0:ds, 0:128], A32("proj", 1, ds), idf[:]), inc=tS)
            op("t", lambda e, ptr=ptr: e.transpose(ptr[0:ds, 128:256], A32("proj", 1 + ds, ds), idf[:]), inc=tS)
            wait("s", tS)
            op("s", lambda e, ts=ts, ptr=ptr: e.copy(a16[0:ds, r16["bctB"] + ts * 128:r16["bctB"] + ts * 128 + 128],
                                                     ptr[0:ds, 0:128]), inc=sS)
            tr_ev[ts % 2] = op("s", lambda e, ts=ts, ptr=ptr: e.copy(
                a16[0:ds, r16["bctC"] + ts * 128:r16["bctC"] + ts * 128 + 128],
                ptr[0:ds, 128:256]), inc=sS)
        wait("t", sS)
        proj_done_s = sS.n

        wait("v", sS, proj_done_s)
        op("v", lambda e: e.tensor_tensor_scan(
            lsrow[:], onest[:], larow[:], lsst[:, 0:1], ALU.mult, ALU.add), inc=vS)
        wait("v", vS)
        op("v", lambda e: e.tensor_copy(lsst[:], lsrow[0:1, tile - 1:tile]), inc=vS)
        wait("v", vS)
        wait("t", vS)
        op("t", lambda e: e.matmul(p[5][:, 0:tile], onesr[:], larow[:], start=True, stop=True), inc=tS)
        wait("s", tS)
        if t > 0:
            wait("s", vS, p1_vend[t - 1])
        op("s", lambda e: e.activation(A32("abc", 0, tile), p[5][:, 0:tile], AF.Exp), inc=sS)
        wait("t", sS)
        op("t", lambda e: e.matmul(p[5][:, 0:tile], onesr[:], lsrow[:], start=True, stop=True), inc=tS)
        wait("s", tS)
        if t > 0:
            wait("s", gP)
        op("s", lambda e: e.activation(A16("qb", 0, tile), p[5][:, 0:tile], AF.Exp), inc=sS)
        aq_done_s = sS.n

        if t + 1 < ntiles and not cfg.get("noilv"):
            LN_load(t + 1, 0)
        wait("t", sS, aq_done_s)
        prev_silu = silu_done if t > 0 else {}
        prev_cev = cev if t > 0 else {}
        prev_vend = p1_vend.get(t - 1, 0)
        prev_pool = gP.n
        silu_done = {}
        bev = {}
        cev = {}
        ylk_done = {}
        pool_done = {}
        for m in range(dsl):
            pz = p[m % 2]
            pB = p[3] if m % 2 == 0 else p[5]
            pC = p[4] if m % 2 == 0 else p[6]
            if m == 0:
                wait("v", gs[0])
                wait("v", gs[1])
            if m >= 2:
                wait("t", sS, silu_done[m - 2])
                wait("t", sS, cev[m - 2])
                wait("s", vS, ylk_done[m - 2])
                wait("s", gP, pool_done[m - 2])
            elif t > 0:
                wait("t", sS, prev_silu[dsl - 2 + m])
                wait("t", sS, prev_cev[dsl - 2 + m])
                wait("s", vS, prev_vend)
                wait("s", gP, prev_pool)
            for k in range(km):
                op("t", lambda e, k=k, m=m, pz=pz: e.matmul(
                    pz[:, 0:tile], win_ap(k)[:, di + m * 128:di + (m + 1) * 128],
                    A16("ln0T", k * tile, tile),
                    start=(k == 0), stop=(k == km - 1)),
                    inc=tS if k == km - 1 else None)
            z_mm = tS.n
            op("t", lambda e, m=m, pB=pB: e.matmul(
                pB[:, 0:tile], wb[:, m * 128:(m + 1) * 128], a16[0:ds, r16["bctB"]:r16["bctB"] + tile],
                start=True, stop=True), inc=tS)
            b_mm = tS.n
            op("t", lambda e, m=m, pC=pC: e.matmul(
                pC[:, 0:tile], wc[:, m * 128:(m + 1) * 128], a16[0:ds, r16["bctC"]:r16["bctC"] + tile],
                start=True, stop=True), inc=tS)
            c_mm = tS.n
            szs = A16("sz", (m % 2) * tile, tile)
            wait("s", tS, z_mm)
            silu_done[m] = op("s", lambda e, szs=szs, pz=pz: e.activation(
                szs, pz[:, 0:tile], AF.Silu), inc=sS)
            wait("s", tS, b_mm)
            bev[m] = op("s", lambda e, pB=pB, m=m: e.copy(A16("bbf", (m % 2) * tile, tile), pB[:, 0:tile]), inc=sS)
            wait("s", tS, c_mm)
            cev[m] = op("s", lambda e, pC=pC, m=m: e.copy(A16("cbf", (m % 2) * tile, tile), pC[:, 0:tile]), inc=sS)
            bxs = A16("bx", 0, tile)
            wait("v", sS, bev[m])
            op("v", lambda e, bxs=bxs, m=m: e.tensor_tensor(
                bxs, A16("bbf", (m % 2) * tile, tile), A16("xiT", m * tile, tile), ALU.mult), inc=vS)
            wait("v", vS)
            hts = A32("hT", (m % 2) * tile, tile)
            op("v", lambda e, hts=hts, bxs=bxs, m=m: e.tensor_tensor_scan(
                hts, A32("abc", 0, tile), bxs, scst[:, m:m + 1], ALU.mult, ALU.add), inc=vS)
            wait("v", vS)
            op("v", lambda e, hts=hts, m=m: e.tensor_copy(
                scst[:, m:m + 1], hts[:, tile - 1:tile]), inc=vS)
            wait("v", vS)
            if m >= 4:
                wait("v", gs[(m // 2) % 2])
            wait("v", sS, cev[m])
            op("v", lambda e, hts=hts, m=m: e.tensor_tensor(
                A16("scrA", 0, tile), A16("cbf", (m % 2) * tile, tile), hts, ALU.mult), inc=vS)
            op("v", lambda e, m=m: e.scalar_tensor_tensor(
                A16("scrB", 0, tile), A16("xiT", m * tile, tile), dcol[:, m:m + 1],
                A16("scrA", 0, tile), ALU.mult, ALU.add), inc=vS)
            yls = A16("ylw", (m % 4) * tile, tile)
            op("v", lambda e, yls=yls, szs=szs: e.tensor_tensor(
                yls, A16("scrB", 0, tile), szs, ALU.mult), inc=vS)
            op("v", lambda e, szs=szs, m=m: e.tensor_tensor(
                A16("scrC", 0, tile), A16("cbf", (m % 2) * tile, tile), szs, ALU.mult), inc=vS)
            kws = A16("kw", (m % 4) * tile, tile)
            ylk_done[m] = op("v", lambda e, kws=kws: e.tensor_tensor(
                kws, A16("scrC", 0, tile), A16("qb", 0, tile), ALU.mult), inc=vS)
            pool_done[m] = gP.n
            if m % 2 == 1:
                par = (m // 2) % 2
                wait("g", vS)
                op("g", lambda e, t=t, m=m, par=par: e.dma_start(
                    out=sp_yl[t, :, (m - 1) * tile:(m + 1) * tile],
                    in_=A16("ylw", ((m - 1) % 4) * tile, 2 * tile)), inc=gs[par], k=16)
                op("g", lambda e, t=t, m=m, par=par: e.dma_start(
                    out=sp_k[t, :, (m - 1) * tile:(m + 1) * tile],
                    in_=A16("kw", ((m - 1) % 4) * tile, 2 * tile)), inc=gs[par], k=16)
            stride = max(1, dsl // tsn)
            if cfg.get("noilv"):
                pass
            elif t + 1 < ntiles and m % stride == stride - 1 and m // stride < tsn:
                j = m // stride
                if j + 1 < tsn:
                    LN_load(t + 1, j + 1)
                LN_stats(t + 1, j)
        if cfg.get("noilv") and t + 1 < ntiles:
            for ts_ in range(tsn):
                LN_load(t + 1, ts_)
                LN_stats(t + 1, ts_)
        p1_zmm_t[t] = z_mm
        p1_vend[t] = vS.n
    p1_t = tS.n

    op("v", lambda e: e.tensor_copy(ccsb[:, 0:dsl], scst[:]), inc=vS)
    wait("t", vS)
    op("t", lambda e: e.matmul(p[6][:, 0:1], onesr[:], lsst[:], start=True, stop=True), inc=tS)
    wait("s", tS)
    op("s", lambda e: e.copy(ccsb[:, dsl:dsl + 1], p[6][:, 0:1]), inc=sS)
    wait("g", sS)
    wait("g", vS)
    wait("g", gs[0])
    wait("g", gs[1])
    op("g", lambda e: e.dma_start(out=cc_in_t.ap(), in_=ccsb[:]), inc=gcc, k=16)
    wait("g", gcc)
    op("g", lambda e: e.collective_compute(
        "AllGather", ALU.bypass, replica_groups=[list(range(ncores))],
        ins=[cc_in_t.ap().opt()], outs=[cc_out_t.ap().opt()]), inc=cS)
    wait("g", cS)
    op("g", lambda e: e.dma_start(
        out=agsb[:], in_=cc_out_t.ap().rearrange("r p c -> p r c")), inc=gcc, k=16)

    wait("v", gcc)
    agv = agsb.rearrange("p (r c) -> p r c", c=dsl + 1)
    for j in range(ncores):
        op("v", lambda e, j=j: e.tensor_tensor(
            ctmp8[:], agv[:, :, dsl], cmaske[:, j * ncores:(j + 1) * ncores],
            ALU.mult), inc=vS)
        wait("v", vS)
        op("v", lambda e: e.tensor_reduce(cred[:], ctmp8[:], AX.X, ALU.add), inc=vS)
        wait("s", vS)
        op("s", lambda e: e.activation(cexp[:], cred[:], AF.Exp), inc=sS)
        wait("v", sS)
        op("v", lambda e, j=j: e.tensor_tensor(
            ccoef[:], cexp[:], cmaskm[:, j:j + 1], ALU.mult), inc=vS)
        wait("v", vS)
        op("v", lambda e, j=j: e.tensor_scalar(
            ctmp16[:], agsb[:, j * (dsl + 1):j * (dsl + 1) + dsl],
            ccoef[:, 0:1], None, ALU.mult), inc=vS)
        wait("v", vS)
        op("v", lambda e: e.tensor_tensor(hin[:], hin[:], ctmp16[:], ALU.add), inc=vS)
        wait("v", vS)
    barrier()

    if stop == "xchg":
        for s_ in (ycn, yx[0], yx[1], yst, gw, gs[0], gs[1], gcc):
            wait("y", s_)
        wait("y", cS, cS.n)
        return B.finalize()

    wait("g", tS, p1_t)
    for k in range(km):
        op("g", lambda e, k=k: e.dma_start(
            out=wi_ap(k), in_=wi_in[:, k * f:(k + 1) * f]), inc=gw, k=16)
    wi_gw = gw.n

    wait("y", gs[0])
    wait("y", gs[1])
    xb_rd = {}
    p2a_tile_v = 0
    p2a_tile_t = 0
    for t in range(ntiles):
        if t:
            wait("y", vS, p2a_tile_v)
            wait("y", tS, p2a_tile_t)
        op("y", lambda e, t=t: e.dma_start(
            out=a16[:, yl_in_off:yl_in_off + dsl * tile], in_=sp_yl[t]), inc=yyk, k=16)
        op("y", lambda e, t=t: e.dma_start(
            out=a16[:, k_in_off:k_in_off + dsl * tile], in_=sp_k[t]), inc=yyk, k=16)
        ymm_par = {}
        lnb_par = {}
        sq_par = {}
        stt_done = {}

        def A_stt(ts, t=t):
            par = ts % 2
            yb0 = y2_off + par * dsl * 128
            row = t * tile + ts * 128
            xb = A32("xb", par * dm, dm)
            if t or ts:
                wait("y", vS, xb_rd.get(par, 0))
            op("y", lambda e, row=row, xb=xb: e.dma_start(out=xb, in_=x_in[row:row + 128, :]), inc=yx[par], k=16)
            if ts < 2:
                wait("v", yyk)
            if ts >= 2:
                wait("v", tS, ymm_par[par])
            stt_done[ts] = []
            for m in range(dsl):
                v_ = op("v", lambda e, m=m, ts=ts, yb0=yb0: e.scalar_tensor_tensor(
                    a16[:, yb0 + m * 128:yb0 + (m + 1) * 128],
                    a16[:, k_in_off + m * tile + ts * 128:k_in_off + m * tile + ts * 128 + 128],
                    hin[:, m:m + 1],
                    a16[:, yl_in_off + m * tile + ts * 128:yl_in_off + m * tile + ts * 128 + 128],
                    ALU.mult, ALU.add), inc=vS)
                stt_done[ts].append(v_)

        def B_mm(ts):
            par = ts % 2
            pp = [p[2 * par + n] for n in range(n512)]
            yb0 = y2_off + par * dsl * 128
            if ts >= 2:
                wait("t", vS, lnb_par[par])
                wait("t", vS, sq_par[par][0])
                if sq_par[par][1]:
                    wait("t", sS, sq_par[par][1])
            for m in range(dsl):
                wait("t", vS, stt_done[ts][m])
                for n in range(n512):
                    op("t", lambda e, m=m, n=n, yb0=yb0, pp=pp: e.matmul(
                        pp[n][:, 0:512],
                        a16[:, yb0 + m * 128:yb0 + (m + 1) * 128],
                        wout_ap(m)[:, n * 512:(n + 1) * 512],
                        start=(m == 0), stop=(m == dsl - 1)),
                        inc=tS if (m == dsl - 1 and n == n512 - 1) else None)
            ymm_par[par] = tS.n

        def C_ln(ts, t=t):
            par = ts % 2
            pp = [p[2 * par + n] for n in range(n512)]
            wo_mm = ymm_par[par]

            def _w3(wo_mm=wo_mm):
                wait("v", tS, wo_mm)
            sq_par[par] = emit_ln_stats([(pp[n][:, 0:512], 512) for n in range(n512)],
                                        pre_wait=_w3)
            wait("v", sS)
            for n in range(n512):
                op("v", lambda e, n=n, pp=pp: e.tensor_scalar(
                    A32("lnb", n * 512, 512), pp[n][:, 0:512], A32("stat", 4, 1), A32("stat", 9, 1),
                    ALU.subtract, ALU.mult), inc=vS)
            lnb_par[par] = vS.n
            xb = A32("xb", par * dm, dm)
            wait("v", yx[par])
            wait("v", yst)
            op("v", lambda e, xb=xb: e.tensor_tensor(
                A32("fout", 0, dm), xb, A32("lnb", 0, dm), ALU.add), inc=vS)
            xb_rd[par] = vS.n
            wait("y", vS)
            op("y", lambda e, t=t, ts=ts: e.dma_start(
                out=sp_x1.ap()[t * tsn + ts], in_=A32("fout", 0, dm)), inc=yst, k=16)

        A_stt(0)
        if tsn > 1:
            A_stt(1)
        for ts in range(tsn):
            B_mm(ts)
            C_ln(ts)
            if ts + 2 < tsn:
                A_stt(ts + 2)
        p2a_tile_v = vS.n
        p2a_tile_t = tS.n
    p2a_yst = yst.n
    p2a_t = tS.n
    if stop == "p2a":
        for s_ in (ycn, yx[0], yx[1], yyk, yst, gw, gs[0], gs[1], gcc):
            wait("y", s_)
        wait("y", cS, cS.n)
        return B.finalize()

    wait("g", tS, p2a_t)
    for k in range(wo_in_b):
        op("g", lambda e, k=k: e.dma_start(
            out=wo_ap(k), in_=wo_in[:, k * dm:(k + 1) * dm]), inc=gw, k=16)

    wait("t", gw, wi_gw)
    wait("y", yst, p2a_yst)
    wi_x1_rd = 0
    wi_lnf_t = {}
    for t in range(ntiles):
        lnf0 = r16["ln0T"] if t % 2 == 0 else r16["ylw"]
        um0 = (t % 2) * tile
        if t:
            wait("y", vS, wi_x1_rd)
        op("y", lambda e, t=t: e.dma_start(
            out=A32("x1", 0, tsn * dm).rearrange("p (r c) -> p r c", c=dm),
            in_=sp_x1.ap()[t * tsn:(t + 1) * tsn].rearrange("r p c -> p r c")), inc=yxl, k=16)
        if t >= 2:
            wait("t", tS, wi_lnf_t[t % 2])
        for ts in range(tsn):
            x1b = A32("x1", ts * dm, dm)
            def _w2():
                wait("v", yxl)
            emit_ln_stats([(x1b, dm)], pre_wait=_w2)
            wait("v", sS)
            wait("v", yxl)
            op("v", lambda e, x1b=x1b: e.tensor_scalar(
                A16("lntok", 0, dm), x1b, A32("stat", 4, 1), A32("stat", 9, 1),
                ALU.subtract, ALU.mult), inc=vS)
            wait("t", vS)
            for g0 in range(0, km, 4):
                gn = min(4, km - g0)
                for j in range(gn):
                    op("t", lambda e, j=j, g0=g0: e.transpose(
                        pT[:, j * 128:(j + 1) * 128],
                        A16("lntok", (g0 + j) * 128, 128), idb[:]),
                        inc=tS if j == gn - 1 else None)
                wait("s", tS)

                def evac_lnf(e, g0=g0, gn=gn, ts=ts, lnf0=lnf0):
                    dst = a16[:, lnf0 + g0 * tile:
                              lnf0 + (g0 + gn) * tile].rearrange(
                        "p (k n) -> p k n", n=tile)[:, :, ts * 128:(ts + 1) * 128]
                    src = pT.rearrange("p (k n) -> p k n", n=128)[:, 0:gn, :]
                    return e.tensor_copy(dst, src)
                wait("v", tS)
                op("v", evac_lnf, inc=vS)
                wait("t", vS)
            wait("t", vS)
            for k in range(km):
                op("t", lambda e, k=k, ts=ts, lnf0=lnf0: e.matmul(
                    p[2][:, 0:e_], a16[:, lnf0 + k * tile + ts * 128:lnf0 + k * tile + ts * 128 + 128],
                    gatet[:, k * e_:(k + 1) * e_],
                    start=(k == 0), stop=(k == km - 1)),
                    inc=tS if k == km - 1 else None)
            wait("v", tS)
            op("v", lambda e: e.tensor_tensor(lg8[:], p[2][:, 0:e_], gateb[:], ALU.add), inc=vS)
            wait("v", vS)
            op("v", lambda e: e.tensor_reduce(mx1[:], lg8[:], AX.X, ALU.max), inc=vS)
            wait("v", vS)
            op("v", lambda e: e.tensor_scalar(pr8[:], lg8[:], mx1[:, 0:1], None, ALU.subtract), inc=vS)
            wait("v", vS)
            wait("s", vS)
            op("s", lambda e: e.activation(eq8[:], pr8[:], AF.Exp, scale=BIG), inc=sS)
            wait("v", sS)
            op("v", lambda e: e.tensor_tensor(pr8[:], eq8[:], iotar[:], ALU.mult), inc=vS)
            wait("v", vS)
            op("v", lambda e: e.tensor_reduce(mx1[:], pr8[:], AX.X, ALU.max), inc=vS)
            wait("v", vS)
            op("v", lambda e: e.tensor_scalar(pr8b[:], pr8[:], mx1[:, 0:1], None, ALU.subtract), inc=vS)
            wait("s", vS)
            op("s", lambda e: e.activation(oh8[:], pr8b[:], AF.Exp, scale=BIG), inc=sS)
            wait("v", sS)
            oh_v = vS.n
            if dbg:
                wait("y", vS)
                op("y", lambda e, t=t, ts=ts: e.dma_start(
                    out=sp_dbg.ap()[t, ts, :, 0:e_], in_=lg8[:]), inc=yst, k=16)
                op("y", lambda e, t=t, ts=ts: e.dma_start(
                    out=sp_dbg.ap()[t, ts, :, e_:2 * e_], in_=eq8[:]), inc=yst, k=16)
                op("y", lambda e, t=t, ts=ts: e.dma_start(
                    out=sp_dbg.ap()[t, ts, :, 2 * e_:3 * e_], in_=oh8[:]), inc=yst, k=16)
                op("y", lambda e, t=t, ts=ts: e.dma_start(
                    out=sp_dbg.ap()[t, ts, :, 3 * e_:3 * e_ + 1], in_=mx1[:, 0:1],
                    allow_slow_non_contiguous=True), inc=yst, k=16)
            wait("t", vS)
            for k in range(km):
                op("t", lambda e, k=k, ts=ts, lnf0=lnf0: e.matmul(
                    p[3][:, 0:er], a16[:, lnf0 + k * tile + ts * 128:lnf0 + k * tile + ts * 128 + 128],
                    lorat[:, k * er:(k + 1) * er],
                    start=(k == 0), stop=(k == km - 1)),
                    inc=tS if k == km - 1 else None)
            wait("v", tS)
            wait("v", vS, oh_v)
            for i in range(e_):
                op("v", lambda e, i=i: e.tensor_scalar(
                    umb[:, i * r_:(i + 1) * r_], p[3][:, i * r_:(i + 1) * r_],
                    oh8[:, i:i + 1], None, ALU.mult), inc=vS)
            wait("t", vS)
            op("t", lambda e: e.transpose(pT[0:er, 0:128], umb[:], idb[:]), inc=tS)
            wait("v", tS)
            op("v", lambda e, ts=ts, um0=um0: e.tensor_copy(umt[:, um0 + ts * 128:um0 + (ts + 1) * 128], pT[0:er, 0:128]), inc=vS)
            wait("t", vS)
        lnf_done_v = vS.n
        wi_x1_rd = vS.n
        wait("t", vS, lnf_done_v)
        relu_done = {}
        for mfm in range(fm):
            pb = p[mfm % 2]
            if mfm >= 2:
                wait("t", sS, relu_done[mfm - 2])
            for k in range(km):
                op("t", lambda e, k=k, mfm=mfm, pb=pb, lnf0=lnf0: e.matmul(
                    pb[:, 0:tile], wi_ap(k)[:, mfm * 128:(mfm + 1) * 128],
                    a16[:, lnf0 + k * tile:lnf0 + (k + 1) * tile],
                    start=(k == 0), stop=False))
            op("t", lambda e, mfm=mfm, pb=pb, um0=um0: e.matmul(
                pb[:, 0:tile], lorab[:, mfm * 128:(mfm + 1) * 128], umt[:, um0:um0 + tile],
                start=False, stop=True), inc=tS)
            wait("s", tS)
            if mfm >= 2:
                wait("s", gi[mfm % 2])
            ists = A16("ist", (mfm % 2) * tile, tile)
            relu_done[mfm] = op("s", lambda e, ists=ists, mfm=mfm, pb=pb: e.activation(
                ists, pb[:, 0:tile], AF.Relu, bias=wibcol[:, mfm:mfm + 1]), inc=sS)
            wait("g", sS)
            op("g", lambda e, t=t, mfm=mfm, ists=ists: e.dma_start(
                out=sp_int.ap()[t, :, mfm, :], in_=ists), inc=gi[mfm % 2], k=16)
        wi_lnf_t[t % 2] = tS.n
    p2bwi_t = tS.n

    wait("g", tS, p2bwi_t)
    for k in range(wo_in_b, fm):
        op("g", lambda e, k=k: e.dma_start(
            out=wo_ap(k), in_=wo_in[:, k * dm:(k + 1) * dm]), inc=gw, k=16)

    wait("t", gw)
    wait("y", gi[0])
    wait("y", gi[1])
    wo_tile_t = 0
    wo_tile_v = 0
    for t in range(ntiles):
        if t:
            wait("y", tS, wo_tile_t)
            wait("y", vS, wo_tile_v)
        ck = fm // 4
        for c in range(4):
            op("y", lambda e, t=t, c=c, ck=ck: e.dma_start(
                out=a16[:, c * ck * tile:(c + 1) * ck * tile],
                in_=sp_int.ap()[t][:, c * ck:(c + 1) * ck, :].rearrange(
                    "p k n -> p (k n)")), inc=yic[c], k=16)
        op("y", lambda e, t=t: e.dma_start(
            out=A32("x1", 0, tsn * dm).rearrange("p (r c) -> p r c", c=dm),
            in_=sp_x1.ap()[t * tsn:(t + 1) * tsn].rearrange("r p c -> p r c")), inc=yxl, k=16)
        fo_par = {}
        for ts in range(tsn):
            par = ts % 2
            pp = [p[2 * par + n] for n in range(n512)]
            wait("v", yxl)
            if ts >= 2:
                wait("t", vS, fo_par[par])
            for n in range(n512):
                for k in range(fm):
                    if ts == 0 and n == 0 and k % ck == 0:
                        wait("t", yic[k // ck])
                    op("t", lambda e, n=n, k=k, ts=ts, pp=pp: e.matmul(
                        pp[n][:, 0:512],
                        a16[:, k * tile + ts * 128:k * tile + ts * 128 + 128],
                        wo_ap(k)[:, n * 512:(n + 1) * 512],
                        start=(k == 0), stop=(k == fm - 1)),
                        inc=tS if k == fm - 1 else None)
            wo_mm = tS.n
            wait("v", tS, wo_mm)
            for n in range(n512):
                op("v", lambda e, n=n, pp=pp: e.tensor_tensor(A32("fout", n * 512, 512), pp[n][:, 0:512],
                                                              wobb[:, n * 512:(n + 1) * 512], ALU.add), inc=vS)
            fo_par[par] = vS.n
            emit_ln_stats([(A32("fout", 0, dm), dm)])
            wait("v", sS)
            ob = A32("outb", 0, dm)
            op("v", lambda e: e.tensor_scalar(
                A32("lnb", 0, dm), A32("fout", 0, dm), A32("stat", 4, 1), A32("stat", 9, 1),
                ALU.subtract, ALU.mult), inc=vS)
            wait("v", yst)
            op("v", lambda e, ob=ob, ts=ts: e.tensor_tensor(
                ob, A32("x1", ts * dm, dm), A32("lnb", 0, dm), ALU.add), inc=vS)
            wait("y", vS)
            row = t * tile + ts * 128
            op("y", lambda e, row=row, ob=ob: e.dma_start(
                out=o_out[row:row + 128, :], in_=ob), inc=yst, k=16)
            if ts == tsn - 1:
                wo_tile_t = tS.n
                wo_tile_v = vS.n

    for s_ in (ycn, yx[0], yx[1], yyk, yxl, yic[0], yic[1], yic[2], yic[3],
               yst, gw, gs[0], gs[1], gi[0], gi[1], gcc):
        wait("y", s_)
    wait("y", cS, cS.n)
    return B.finalize()


def make_core_inputs(cfg, inp, core, shared_cache):
    dm, di, ds, f, e_, r_ = (cfg[k] for k in ["dm", "di", "ds", "f", "e", "r"])
    nt, ncores, nbatch = cfg["nt"], cfg["ncores"], cfg["nbatch"]
    km, dsl, fm, er = dm // 128, di // 128, f // 128, e_ * r_
    chain = ncores // nbatch

    C = np.ascontiguousarray

    def ktile(w, kdim):
        n = w.shape[1]
        return C(w.reshape(kdim, 128, n).transpose(1, 0, 2).reshape(128, kdim * n))

    if not shared_cache:
        shared_cache["x"] = np.asarray(inp["L0_attn_x"], np.float32).reshape(ncores, nt, dm)
        W_in = np.asarray(inp["L0_attn_W_in"], np.float32)
        W_xp = np.asarray(inp["L0_attn_W_xp"], np.float32)
        W_out = np.asarray(inp["L0_attn_W_out"], np.float32)
        wi_W = np.asarray(inp["L0_ffn_wi_W"], np.float32)
        wo_W = np.asarray(inp["L0_ffn_wo_W"], np.float32)
        gate_W = np.asarray(inp["L0_ffn_gate_W"], np.float32)
        lora_A = np.asarray(inp["L0_ffn_lora_A"], np.float32)
        lora_B = np.asarray(inp["L0_ffn_lora_B"], np.float32)
        D = np.asarray(inp["L0_attn_D"], np.float32)
        wi_b = np.asarray(inp["L0_ffn_wi_b"], np.float32)
        wo_b = np.asarray(inp["L0_ffn_wo_b"], np.float32)
        gate_b = np.asarray(inp["L0_ffn_gate_b"], np.float32)
        shared_cache["w"] = {
            "w_win": ktile(W_in, km),
            "w_wxp": ktile(W_xp, dsl),
            "w_wb": C(np.asarray(inp["L0_attn_W_B"], np.float32)),
            "w_wc": C(np.asarray(inp["L0_attn_W_C"], np.float32)),
            "w_wout": ktile(W_out, dsl),
            "w_wi": ktile(C(wi_W.T), km),
            "w_wo": ktile(C(wo_W.T), fm),
            "w_gatet": ktile(C(gate_W.T), km),
            "w_lorat": ktile(C(lora_A.reshape(er, dm).T), km),
            "w_lorab": C(lora_B.transpose(0, 2, 1).reshape(er, f)),
            "w_dcol": C(D.reshape(dsl, 128).T),
            "w_wibcol": C(wi_b.reshape(fm, 128).T),
            "w_wobb": C(np.broadcast_to(wo_b, (128, dm))),
            "w_gateb": C(np.broadcast_to(gate_b, (128, e_))),
            "w_iotar": C(np.broadcast_to((e_ - np.arange(e_, dtype=np.float32)).reshape(1, e_), (128, e_))),
            "w_alog": np.asarray(inp["L0_attn_A_log"], np.float32).reshape(1, 1),
            "w_ident": np.eye(128, dtype=np.float32),
            "w_onesr": np.ones((1, 128), np.float32),
        }
    b0 = (core // chain) * chain
    me = np.zeros((ncores, ncores), np.float32)
    mmv = np.zeros((ncores,), np.float32)
    for j in range(b0, core):
        mmv[j] = 1.0
        for k in range(j + 1, core):
            me[j, k] = 1.0
    d = dict(shared_cache["w"])
    d["x"] = C(shared_cache["x"][core])
    d["w_cmaske"] = C(np.broadcast_to(me.reshape(1, -1), (128, ncores * ncores)))
    d["w_cmaskm"] = C(np.broadcast_to(mmv.reshape(1, -1), (128, ncores)))
    return d


_PROGRAM_CACHE = {}


def kernel(**inputs):
    cfg = default_cfg()
    if "main" not in _PROGRAM_CACHE:
        _PROGRAM_CACHE["main"] = build_program(cfg)
    nc = _PROGRAM_CACHE["main"]
    ncores = cfg["ncores"]
    shared = {}
    in_maps = [make_core_inputs(cfg, inputs, c, shared) for c in range(ncores)]
    res = run_bass_kernel_spmd(nc, in_maps, core_ids=list(range(ncores)))
    nt, dm = cfg["nt"], cfg["dm"]
    full = np.empty((ncores, nt, dm), np.float32)
    for c in range(ncores):
        full[c] = res.results[c]["o"].reshape(nt, dm)
    return full.reshape(cfg["nbatch"], (ncores // cfg["nbatch"]) * nt, dm)
